# version 26
# baseline (speedup 1.0000x reference)
import numpy as np
from contextlib import ExitStack
import concourse.bass as bass
import concourse.mybir as mybir
import concourse.bass_utils as _bu
from concourse.bass_utils import run_bass_kernel_spmd

if not getattr(_bu.get_walrus_args, "_ldw_patched", False):
    _orig_gwa = _bu.get_walrus_args

    def _gwa_ldw(*a, **k):
        return [x.replace("--enable-ldw-opt=false", "--enable-ldw-opt=true")
                if isinstance(x, str) else x for x in _orig_gwa(*a, **k)]

    _gwa_ldw._ldw_patched = True
    _bu.get_walrus_args = _gwa_ldw

F32 = mybir.dt.float32
BF16 = mybir.dt.bfloat16
AF = mybir.ActivationFunctionType
ALU = mybir.AluOpType
AX = mybir.AxisListType

T = 4096
Q = 2048
E = 1024
NH = 16
S = 64
I = 2816
NP = 128
NEC = E // NP
NIC = I // NP
NTT = T // NP
NQT = Q // NP
QT = 512
NQTL = Q // QT
NPAIR = NH // 2

ENGS = ("sync", "gpsimd", "vector", "scalar", "tensor")


class Emit:

    def __init__(self):
        self.ops = {e: [] for e in ENGS}
        self.cnt = {}

    def op(self, eng, fn, inc=None, inc_n=1):
        self.ops[eng].append((fn, inc, inc_n))
        if inc is not None:
            self.cnt[inc] = self.cnt.get(inc, 0) + inc_n
        return self.cnt.get(inc, 0)

    def wait(self, eng, sem, val):
        if val > 0:
            self.ops[eng].append((("W", sem, val), None, 0))

    def wait_now(self, eng, sem):
        self.wait(eng, sem, self.cnt.get(sem, 0))

    def val(self, sem):
        return self.cnt.get(sem, 0)

def flush_block(nc, em, sems):
    ops = em.ops
    em.ops = {e: [] for e in ENGS}
    with nc.Block() as blk:
        def runner(lst):
            def run(e):
                for fn, inc, inc_n in lst:
                    if isinstance(fn, tuple) and fn[0] == "W":
                        e.wait_ge(sems[fn[1]], fn[2])
                    else:
                        ins = fn(e)
                        if inc is not None:
                            ins.then_inc(sems[inc], inc_n)
            return run
        if ops["sync"]:
            blk.sync(runner(ops["sync"]))
        if ops["gpsimd"]:
            blk.gpsimd(runner(ops["gpsimd"]))
        if ops["vector"]:
            blk.vector(runner(ops["vector"]))
        if ops["scalar"]:
            blk.scalar(runner(ops["scalar"]))
        if ops["tensor"]:
            blk.tensor(runner(ops["tensor"]))


def emit_quant(em, x_ap, out_ap, B, P, Qb, R, sgn, amax, s_, r_, N,
               x_alias_p=False, warm_cb=None):
    nb = N // 16
    x3 = x_ap.rearrange("p (b k) -> p b k", k=16)
    rb = s_[:, 0:nb].unsqueeze(2).broadcast_to([128, nb, 16])
    qb = r_[:, 0:nb].unsqueeze(2).broadcast_to([128, nb, 16])

    nv = [0]

    def V(fn):
        return em.op("vector", fn, "vs")

    def VW():
        em.wait_now("vector", "vs")
        nv[0] += 1
        if warm_cb is not None and nv[0] % 2 == 0:
            warm_cb()

    em.wait_now("scalar", "vs")
    em.op("scalar", lambda e: e.activation(B[:, 0:N], x_ap, AF.Abs), "ss")
    sc_abs = em.val("ss")
    em.op("scalar", lambda e: e.activation(sgn[:, 0:N], x_ap, AF.Sign), "ss")
    sc_sgn = em.val("ss")
    V(lambda e: e.tensor_reduce(amax[:, 0:nb], x3, AX.X, ALU.max,
                                apply_absolute_value=True)); VW()
    V(lambda e: e.reciprocal(r_[:, 0:nb], amax[:, 0:nb])); VW()
    V(lambda e: e.tensor_scalar(s_[:, 0:nb], r_[:, 0:nb], 6.0, 1e12,
                                ALU.mult, ALU.min)); VW()
    V(lambda e: e.reciprocal(r_[:, 0:nb], s_[:, 0:nb])); VW()
    em.wait("vector", "ss", sc_sgn if x_alias_p else sc_abs)
    V(lambda e: e.tensor_tensor(
        P[:, 0:N].rearrange("p (b k) -> p b k", k=16),
        B[:, 0:N].rearrange("p (b k) -> p b k", k=16), rb, ALU.mult)); VW()
    V(lambda e: e.tensor_single_scalar(Qb[:, 0:N], P[:, 0:N], 0.25, ALU.is_gt)); VW()
    V(lambda e: e.scalar_tensor_tensor(R[:, 0:N], P[:, 0:N], 0.75, Qb[:, 0:N],
                                       ALU.is_gt, ALU.add)); VW()
    V(lambda e: e.scalar_tensor_tensor(Qb[:, 0:N], P[:, 0:N], 1.25, R[:, 0:N],
                                       ALU.is_gt, ALU.add)); VW()
    V(lambda e: e.scalar_tensor_tensor(R[:, 0:N], P[:, 0:N], 1.75, Qb[:, 0:N],
                                       ALU.is_gt, ALU.add)); VW()
    V(lambda e: e.tensor_single_scalar(Qb[:, 0:N], P[:, 0:N], 2.5, ALU.is_gt)); VW()
    V(lambda e: e.scalar_tensor_tensor(B[:, 0:N], P[:, 0:N], 3.5, Qb[:, 0:N],
                                       ALU.is_gt, ALU.add)); VW()
    V(lambda e: e.scalar_tensor_tensor(Qb[:, 0:N], P[:, 0:N], 5.0, B[:, 0:N],
                                       ALU.is_gt, ALU.add)); VW()
    V(lambda e: e.scalar_tensor_tensor(B[:, 0:N], P[:, 0:N], 5.0, Qb[:, 0:N],
                                       ALU.is_gt, ALU.add)); VW()
    V(lambda e: e.scalar_tensor_tensor(P[:, 0:N], R[:, 0:N], 0.5, B[:, 0:N],
                                       ALU.mult, ALU.add)); VW()
    em.wait("vector", "ss", sc_sgn)
    V(lambda e: e.tensor_tensor(Qb[:, 0:N], P[:, 0:N], sgn[:, 0:N], ALU.mult)); VW()
    V(lambda e: e.tensor_tensor(
        out_ap.rearrange("p (b k) -> p b k", k=16),
        Qb[:, 0:N].rearrange("p (b k) -> p b k", k=16), qb, ALU.mult)); VW()


def emit_ln_stats(em, x_ap, sq_scr, lnm, lnv, lnr1, lnr2, ceps, n=E):
    em.op("vector", lambda e: e.reduce_sum(lnv[:], x_ap, AX.X), "vs")
    em.wait_now("vector", "vs")
    em.op("vector", lambda e: e.tensor_scalar_mul(lnm[:], lnv[:], 1.0 / n), "vs")
    em.wait_now("scalar", "vs")
    em.op("scalar", lambda e: e.activation(sq_scr[:, 0:n], x_ap, AF.Square,
                                           bias=lnm[:], scale=-1.0,
                                           accum_out=lnv[:]), "ss")
    em.wait_now("scalar", "ss")
    em.op("scalar", lambda e: e.activation(lnr1[:], lnv[:], AF.Ln,
                                           bias=ceps[:], scale=1.0 / n), "ss")
    em.wait_now("scalar", "ss")
    em.op("scalar", lambda e: e.activation(lnr2[:], lnr1[:], AF.Exp,
                                           scale=-0.5), "ss")
    return em.val("ss")


def emit_ln_apply(em, x_ap, out_ap, lnm, lnr2, ss_done):
    em.wait("vector", "ss", ss_done)
    em.op("vector", lambda e: e.tensor_scalar(out_ap, x_ap, lnm[:], lnr2[:],
                                              ALU.subtract, ALU.mult), "vs")
    em.wait_now("vector", "vs")


def emit_ln(em, x_ap, out_ap, sq_scr, lnm, lnv, lnr1, lnr2, ceps, n=E):
    em.op("vector", lambda e: e.reduce_sum(lnv[:], x_ap, AX.X), "vs")
    em.wait_now("vector", "vs")
    em.op("vector", lambda e: e.tensor_scalar_mul(lnm[:], lnv[:], 1.0 / n), "vs")
    em.wait_now("vector", "vs")
    em.wait_now("scalar", "vs")
    em.op("scalar", lambda e: e.activation(sq_scr[:, 0:n], x_ap, AF.Square,
                                           bias=lnm[:], scale=-1.0,
                                           accum_out=lnv[:]), "ss")
    em.wait_now("scalar", "ss")
    em.op("scalar", lambda e: e.activation(lnr1[:], lnv[:], AF.Ln,
                                           bias=ceps[:], scale=1.0 / n), "ss")
    em.wait_now("scalar", "ss")
    em.op("scalar", lambda e: e.activation(lnr2[:], lnr1[:], AF.Exp,
                                           scale=-0.5), "ss")
    em.wait_now("vector", "ss")
    em.op("vector", lambda e: e.tensor_scalar(out_ap, x_ap, lnm[:], lnr2[:],
                                              ALU.subtract, ALU.mult), "vs")
    em.wait_now("vector", "vs")


def build_nc():
    nc = bass.Bass("TRN2", target_bir_lowering=False,
                   detect_race_conditions=False)
    xb = nc.dram_tensor("xb", [T, E], F32, kind="ExternalInput").ap()
    wkT = nc.dram_tensor("wkT", [E, E], F32, kind="ExternalInput").ap()
    wqT = nc.dram_tensor("wqT", [E, E], F32, kind="ExternalInput").ap()
    wvT = nc.dram_tensor("wvT", [E, E], F32, kind="ExternalInput").ap()
    wuT = nc.dram_tensor("wuT", [E, E], F32, kind="ExternalInput").ap()
    bu = nc.dram_tensor("bu", [1, E], F32, kind="ExternalInput").ap()
    wg = nc.dram_tensor("wg", [I, E], F32, kind="ExternalInput").ap()
    wu2 = nc.dram_tensor("wu2", [I, E], F32, kind="ExternalInput").ap()
    wd = nc.dram_tensor("wd", [E, I], F32, kind="ExternalInput").ap()
    wgs = nc.dram_tensor("wgs", [3 * NP, E], F32, kind="ExternalInput").ap()
    wus = nc.dram_tensor("wus", [3 * NP, E], F32, kind="ExternalInput").ap()
    wds = nc.dram_tensor("wds", [1 * NP, I], F32, kind="ExternalInput").ap()
    y = nc.dram_tensor("y", [Q, E], F32, kind="ExternalOutput").ap()
    x2s = nc.dram_tensor("x2s", [Q, E], F32).ap()
    xls = nc.dram_tensor("xls", [NEC, NP, T], BF16).ap()
    wg_loc = nc.dram_tensor("wg_loc", [3, E, NP], BF16).ap()
    wu_loc = nc.dram_tensor("wu_loc", [3, E, NP], BF16).ap()
    wd_loc = nc.dram_tensor("wd_loc", [1, I, NP], BF16).ap()
    wgg = nc.dram_tensor("wgg", [24, E, NP], BF16).ap()
    wug = nc.dram_tensor("wug", [24, E, NP], BF16).ap()
    wdg = nc.dram_tensor("wdg", [8, I, NP], BF16).ap()

    em = Emit()
    es = ExitStack()
    sems = {}

    def sem(name):
        sems[name] = es.enter_context(nc.semaphore(name))

    def sb(scope, name, shape, dt=F32):
        return scope.enter_context(nc.sbuf_tensor(name, shape, dt))

    with es:
        for s in ("lx0", "lx1", "gw", "lw", "lxr", "sx2", "sy",
                  "vs", "ss", "ts", "gs", "sw", "lwq", "gww"):
            sem(s)
        stqA = es.enter_context(nc.psum_tensor("stqA", [NP, 1024], F32))
        stqB = es.enter_context(nc.psum_tensor("stqB", [NP, 1024], F32))
        avP = es.enter_context(nc.psum_tensor("avP", [NP, 512], F32))
        zP = es.enter_context(nc.psum_tensor("zP", [NP, 512], F32))
        zrepP = es.enter_context(nc.psum_tensor("zrepP", [NP, 512], F32))
        tpP = es.enter_context(nc.psum_tensor("tpP", [NP, 1024], BF16))
        identb = sb(es, "identb", [NP, NP], BF16)
        onestmp = sb(es, "onestmp", [NP, NP], BF16)
        onesz = sb(es, "onesz", [NP, 2], BF16)
        sel33 = sb(es, "sel33", [33, NP])
        cbexp = sb(es, "cbexp", [NP, 1])
        csexp = sb(es, "csexp", [NP, 1])
        ceps = sb(es, "ceps", [NP, 1])
        lnm = sb(es, "lnm", [NP, 1])
        lnv = sb(es, "lnv", [NP, 1])
        lnr1 = sb(es, "lnr1", [NP, 1])
        lnr2 = sb(es, "lnr2", [NP, 1])
        lnmB = sb(es, "lnmB", [NP, 1])
        lnvB = sb(es, "lnvB", [NP, 1])
        lnr1B = sb(es, "lnr1B", [NP, 1])
        lnr2B = sb(es, "lnr2B", [NP, 1])

        def G(fn):
            return em.op("gpsimd", fn, "gs")

        G(lambda e: e.memset(onestmp[:], 1.0))
        G(lambda e: e.memset(onesz[:], 1.0))
        G(lambda e: e.memset(sel33[:], 0.0))
        G(lambda e: e.memset(cbexp[:], -16.0))
        G(lambda e: e.memset(csexp[:], 0.125))
        G(lambda e: e.memset(ceps[:], 1e-5))
        em.wait_now("gpsimd", "gs")
        G(lambda e: e.affine_select(identb[:], onestmp[:], [[1, NP]],
                                    ALU.is_equal, 0.0, channel_multiplier=-1))
        G(lambda e: e.memset(sel33[0:1, 0:64], 1.0))
        G(lambda e: e.memset(sel33[32:33, 64:128], 1.0))
        em.wait_now("gpsimd", "gs")
        flush_block(nc, em, sems)

        with ExitStack() as esPA:
            kT = sb(esPA, "kT", [NP, NEC * T], BF16)
            qT = sb(esPA, "qT", [NP, NEC * Q], BF16)
            vS = None

            def emit_proj_phase(which):
                with ExitStack() as esP:
                    if which == "kq":
                        wk_sb = sb(esP, "wk_sb", [NP, NEC * E], BF16)
                        wq_sb = sb(esP, "wq_sb", [NP, NEC * E], BF16)
                        win = sb(esP, "win", [NP, I])
                        wqbf = sb(esP, "wqbf", [NP, I], BF16)
                        wstage = sb(esP, "wstage", [NP, I], BF16)
                        qB = sb(esP, "qB", [NP, 1408])
                        qP = sb(esP, "qP", [NP, 1408])
                        qQ = sb(esP, "qQ", [NP, 1408])
                        qR = sb(esP, "qR", [NP, 1408])
                        qsgn = sb(esP, "qsgn", [NP, 1408], BF16)
                        qam = sb(esP, "qam", [NP, 88])
                        qs = sb(esP, "qs", [NP, 88])
                        qr = sb(esP, "qr", [NP, 88])
                        wjobs = ([(wgs, wg_loc, t, 3, E) for t in range(3)]
                                 + [(wus, wu_loc, t, 3, E) for t in range(3)]
                                 + [(wds, wd_loc, t, 1, I) for t in range(1)])
                        wstate = {"win": [0, 0], "stage": [0, 0]}

                        def emit_wtile(idx, tpst):
                            srcw, dstw, t, ntile, N = wjobs[idx]
                            slot = 0
                            em.wait("scalar", "vs", wstate["win"][slot])
                            em.op("scalar", lambda e, srcw=srcw, t=t, N=N,
                                  slot=slot: e.dma_start(
                                      out=win[:, 0:N],
                                      in_=srcw[NP * t:NP * (t + 1), 0:N]),
                                  "lwq", 16)
                            ld = em.val("lwq")
                            em.wait("vector", "lwq", ld)
                            em.wait("scalar", "lwq", ld)
                            nch = (N + 1407) // 1408
                            win_free = None
                            for ci in range(nch):
                                c0 = ci * 1408
                                w = min(1408, N - c0)
                                vs0 = em.val("vs")
                                emit_quant(em,
                                           win[:, c0:c0 + w],
                                           wqbf[:, c0:c0 + w],
                                           qB, qP, qQ, qR, qsgn,
                                           qam, qs, qr, w)
                                win_free = vs0 + 1
                            wstate["win"][slot] = win_free
                            qv = em.val("vs")
                            nct = N // NP
                            for j0 in range(0, nct, 8):
                                jn = min(8, nct - j0)
                                em.wait("tensor", "vs", qv)
                                em.wait("tensor", "vs", tpst["evac"])
                                if j0 == 0:
                                    em.wait("tensor", "gww",
                                            wstate["stage"][0])
                                for jj in range(jn):
                                    j = j0 + jj
                                    em.op("tensor", lambda e, j=j, jj=jj:
                                          e.transpose(
                                              tpP[:, NP * jj:NP * (jj + 1)],
                                              wqbf[:, NP * j:NP * (j + 1)],
                                              identb[:]), "ts")
                                tdone = em.val("ts")
                                em.wait("vector", "ts", tdone)
                                em.wait("vector", "gww", wstate["stage"][0])
                                em.op("vector", lambda e, j0=j0, jn=jn:
                                      e.tensor_copy(
                                          wstage[:, NP * j0:NP * (j0 + jn)],
                                          tpP[:, 0:NP * jn]), "vs")
                                tpst["evac"] = em.val("vs")
                            em.wait("gpsimd", "vs", tpst["evac"])
                            em.op("gpsimd", lambda e, dstw=dstw, t=t, N=N:
                                  e.dma_start(
                                      out=dstw[t, 0:N, :]
                                      .rearrange("(j p) c -> p j c", p=NP),
                                      in_=wstage[:, 0:N]
                                      .rearrange("p (j c) -> p j c", c=NP)),
                                  "gww", 16)
                            wstate["stage"][0] = em.val("gww")
                    else:
                        wv_sb = sb(esP, "wv_sb", [NP, NEC * E], BF16)
                    xin = sb(esP, "xin_" + which, [NP, 2 * E])
                    xlbf = sb(esP, "xlbf_" + which, [NP, E], BF16)
                    xlT = sb(esP, "xlT_" + which, [NP, 4 * E], BF16)
                    lnSq = sb(esP, "lnSq_" + which, [NP, E])
                    if which == "kq":
                        for p in range(NEC):
                            em.op("gpsimd", lambda e, p=p: e.dma_start(
                                out=wk_sb[:, E * p:E * (p + 1)],
                                in_=wkT[NP * p:NP * (p + 1), :]), "gw", 16)
                            em.op("gpsimd", lambda e, p=p: e.dma_start(
                                out=wq_sb[:, E * p:E * (p + 1)],
                                in_=wqT[NP * p:NP * (p + 1), :]), "gw", 16)
                        em.wait_now("gpsimd", "gw")
                        for p in range(NEC):
                            dg = slice(E * p + NP * p, E * p + NP * (p + 1))
                            for wsb, cm, pat in ((wk_sb, 1, -1),
                                                 (wq_sb, -1, 1)):
                                em.op("gpsimd", lambda e, wsb=wsb, dg=dg:
                                      e.tensor_copy(onestmp[:], wsb[:, dg]),
                                      "gs")
                                em.wait_now("gpsimd", "gs")
                                em.op("gpsimd", lambda e, wsb=wsb, dg=dg,
                                      cm=cm, pat=pat: e.affine_select(
                                          wsb[:, dg], onestmp[:],
                                          [[pat, NP]], ALU.is_ge, 0.0,
                                          channel_multiplier=cm), "gs")
                                em.wait_now("gpsimd", "gs")
                        wmask_done = em.val("gs")
                        em.wait("tensor", "gs", wmask_done)
                    else:
                        for p in range(NEC):
                            em.op("gpsimd", lambda e, p=p: e.dma_start(
                                out=wv_sb[:, E * p:E * (p + 1)],
                                in_=wvT[NP * p:NP * (p + 1), :]), "gw", 16)
                        em.wait("tensor", "gw", em.val("gw"))
                    kq_evac = [0, 0]
                    q_evac = [0, 0]
                    v_evac = [0, 0]
                    tpst = {"evac": 0}
                    widx = [0]
                    xlT_used = [0, 0]

                    def do_apply(i, slot, pset, sdone):
                        emit_ln_apply(em, xin[:, E * slot:E * (slot + 1)],
                                      xlbf[:, 0:E], pset[0], pset[3], sdone)
                        xin_used[slot] = em.val("vs")
                        em.wait_now("tensor", "vs")
                        em.wait("tensor", "vs", tpst["evac"])
                        for p in range(NEC):
                            em.op("tensor", lambda e, p=p: e.transpose(
                                tpP[:, NP * p:NP * (p + 1)],
                                xlbf[:, NP * p:NP * (p + 1)],
                                identb[:]), "ts")
                        tdone = em.val("ts")
                        em.wait("vector", "ts", tdone)
                        em.wait("vector", "ts", xlT_used[0])
                        em.op("vector", lambda e, i=i: e.tensor_copy(
                            xlT[:, 0:4 * E]
                            .rearrange("p (c q) -> p c q", q=512)
                            [:, :, NP * i:NP * (i + 1)],
                            tpP[:].rearrange("p (c q) -> p c q", q=NP)),
                            "vs")
                        tpst["evac"] = em.val("vs")
                    lxs = ["lx0", "lx1"]
                    xin_used = [0, 0]
                    for g in range(NEC):
                        gs_ = 0
                        xo = 0
                        pend = None
                        if which == "kq":
                            em.wait("vector", "gww", em.val("gww"))
                        for i in range(4):
                            slot = i % 2
                            pset = ((lnm, lnv, lnr1, lnr2) if i % 2 == 0
                                    else (lnmB, lnvB, lnr1B, lnr2B))
                            em.wait("sync", "vs", xin_used[slot])
                            em.op("sync", lambda e, g=g, i=i, slot=slot:
                                  e.dma_start(
                                      out=xin[:, E * slot:E * (slot + 1)],
                                      in_=xb[512 * g + NP * i:
                                             512 * g + NP * (i + 1), :]),
                                  lxs[slot], 16)
                            ldv = em.val(lxs[slot])
                            em.wait("vector", lxs[slot], ldv)
                            em.wait("scalar", lxs[slot], ldv)
                            sdone = emit_ln_stats(
                                em, xin[:, E * slot:E * (slot + 1)], lnSq,
                                pset[0], pset[1], pset[2], pset[3], ceps)
                            if pend is not None:
                                do_apply(*pend)
                            pend = (i, slot, pset, sdone)
                        do_apply(*pend)
                        xl_ready = em.val("vs")
                        if which == "kq":
                            for m in range(NEC):
                                slot = m % 2
                                kp = stqA[:, 512 * slot:512 * (slot + 1)]
                                em.wait("tensor", "vs", xl_ready)
                                em.wait("tensor", "vs", kq_evac[slot])
                                for k in range(m, NEC):
                                    em.op("tensor", lambda e, m=m, k=k, kp=kp,
                                          xo=xo: e.matmul(
                                              kp,
                                              wk_sb[:, E * k + NP * m:
                                                    E * k + NP * (m + 1)],
                                              xlT[:, xo + 512 * k:
                                                  xo + 512 * (k + 1)],
                                              start=(k == m), stop=(k == NEC - 1)),
                                          "ts" if k == NEC - 1 else None)
                                tdone = em.val("ts")
                                em.wait("vector", "ts", tdone)
                                em.op("vector", lambda e, m=m, g=g, kp=kp:
                                      e.tensor_copy(
                                          kT[:, T * m + 512 * g:
                                             T * m + 512 * (g + 1)], kp), "vs")
                                kq_evac[slot] = em.val("vs")
                                if g < 4:
                                    qp = stqB[:, 512 * slot:512 * (slot + 1)]
                                    em.wait("tensor", "vs", q_evac[slot])
                                    for k in range(0, m + 1):
                                        em.op("tensor",
                                              lambda e, m=m, k=k, qp=qp, xo=xo:
                                              e.matmul(
                                                  qp,
                                                  wq_sb[:, E * k + NP * m:
                                                        E * k + NP * (m + 1)],
                                                  xlT[:, xo + 512 * k:
                                                      xo + 512 * (k + 1)],
                                                  start=(k == 0), stop=(k == m)),
                                              "ts" if k == m else None)
                                    tdone = em.val("ts")
                                    em.wait("vector", "ts", tdone)
                                    em.op("vector", lambda e, m=m, g=g, qp=qp:
                                          e.tensor_copy(
                                              qT[:, Q * m + 512 * g:
                                                 Q * m + 512 * (g + 1)], qp),
                                          "vs")
                                    q_evac[slot] = em.val("vs")
                            em.wait("gpsimd", "vs", em.val("vs"))
                            em.op("gpsimd", lambda e, g=g: e.dma_start(
                                out=xls[0:NEC, :, 512 * g:512 * (g + 1)]
                                .transpose([1, 0, 2]),
                                in_=xlT[:, 0:4 * E]
                                .rearrange("p (c q) -> p c q", q=512)), "gww",
                                16)
                            nW = 1 if g < 7 else len(wjobs) - widx[0]
                            for _ in range(nW):
                                if widx[0] < len(wjobs):
                                    emit_wtile(widx[0], tpst)
                                    widx[0] += 1
                            xlT_used[gs_] = max(em.val("ts"), 0)
        
                        else:
                            for t in range(4):
                                for n in range(2):
                                    slot = n
                                    vp = (avP if slot == 0 else zP)[:, 0:512]
                                    em.wait("tensor", "vs", xl_ready)
                                    em.wait("tensor", "vs", v_evac[slot])
                                    for k in range(NEC):
                                        em.op("tensor",
                                              lambda e, t=t, n=n, k=k, vp=vp,
                                              xo=xo: e.matmul(
                                                  vp,
                                                  xlT[:, xo + 512 * k + NP * t:
                                                      xo + 512 * k + NP * (t + 1)],
                                                  wv_sb[:, E * k + 512 * n:
                                                        E * k + 512 * (n + 1)],
                                                  start=(k == 0),
                                                  stop=(k == NEC - 1)),
                                              "ts" if k == NEC - 1 else None)
                                    tdone = em.val("ts")
                                    em.wait("vector", "ts", tdone)
                                    em.op("vector",
                                          lambda e, g=g, t=t, n=n, vp=vp:
                                          e.tensor_copy(
                                              vS[:, E * (4 * g + t) + 512 * n:
                                                 E * (4 * g + t) + 512 * (n + 1)],
                                              vp), "vs")
                                    v_evac[slot] = em.val("vs")
                            xlT_used[gs_] = em.val("ts")
                    flush_block(nc, em, sems)

            emit_proj_phase("kq")
            esV = esPA.enter_context(ExitStack())
            vS = sb(esV, "vS", [NP, NTT * E], BF16)
            with ExitStack() as esP2:
                wv_sb = sb(esP2, "wv_sb", [NP, NEC * E], BF16)
                xlT2 = sb(esP2, "xlT2", [NP, 2 * 4 * E], BF16)
                for p in range(NEC):
                    em.op("gpsimd", lambda e, p=p: e.dma_start(
                        out=wv_sb[:, E * p:E * (p + 1)],
                        in_=wvT[NP * p:NP * (p + 1), :]), "gw", 16)
                em.wait("tensor", "gw", em.val("gw"))
                em.wait_now("sync", "gww")
                v2_evac = [0, 0]
                xl2_used = [0, 0]
                for g in range(NEC):
                    gsl = g % 2
                    xo2 = 4 * E * gsl
                    em.wait("sync", "ts", xl2_used[gsl])
                    em.op("sync", lambda e, g=g, xo2=xo2: e.dma_start(
                        out=xlT2[:, xo2:xo2 + 4 * E]
                        .rearrange("p (c q) -> p c q", q=512),
                        in_=xls[0:NEC, :, 512 * g:512 * (g + 1)]
                        .transpose([1, 0, 2])), "lx0", 16)
                    ldv = em.val("lx0")
                    em.wait("tensor", "lx0", ldv)
                    for t in range(4):
                        for n in range(2):
                            vslot = n
                            vp = (avP if vslot == 0 else zP)[:, 0:512]
                            em.wait("tensor", "vs", v2_evac[vslot])
                            for k in range(NEC):
                                em.op("tensor",
                                      lambda e, t=t, n=n, k=k, vp=vp, xo2=xo2:
                                      e.matmul(
                                          vp,
                                          xlT2[:, xo2 + 512 * k + NP * t:
                                              xo2 + 512 * k + NP * (t + 1)],
                                          wv_sb[:, E * k + 512 * n:
                                                E * k + 512 * (n + 1)],
                                          start=(k == 0),
                                          stop=(k == NEC - 1)),
                                      "ts" if k == NEC - 1 else None)
                            tdone = em.val("ts")
                            em.wait("vector", "ts", tdone)
                            em.op("vector",
                                  lambda e, g=g, t=t, n=n, vp=vp:
                                  e.tensor_copy(
                                      vS[:, E * (4 * g + t) + 512 * n:
                                         E * (4 * g + t) + 512 * (n + 1)],
                                      vp), "vs")
                            v2_evac[vslot] = em.val("vs")
                    xl2_used[gsl] = em.val("ts")
                flush_block(nc, em, sems)

            with ExitStack() as esA:
                wu_sb = sb(esA, "wu_sb", [NP, NEC * E], BF16)
                ptile = sb(esA, "ptile", [NP, 2 * 1024], BF16)
                aT = sb(esA, "aT", [NP, NEC * QT], BF16)
                zrs = sb(esA, "zrs", [NP, QT])
                zln = sb(esA, "zln", [33, QT])
                zrec = sb(esA, "zrec", [33, QT])
                bubc = sb(esA, "bubc", [NP, E])
                xr_sb = sb(esA, "xr_sb", [NP, E])
                xrb_sb = sb(esA, "xrb_sb", [NP, E])
                x2_sb = xr_sb
                em.op("gpsimd", lambda e: e.memset(zrec[:], 0.0), "gs")
                zrec_zeroed = em.val("gs")
                em.wait("tensor", "gs", zrec_zeroed)
                em.wait_now("gpsimd", "gww")
                for lo, gg in ((wg_loc, wgg), (wu_loc, wug), (wd_loc, wdg)):
                    em.op("gpsimd", lambda e, lo=lo, gg=gg:
                          e.collective_compute(
                              "AllGather", ALU.bypass,
                              replica_groups=[list(range(8))],
                              ins=[lo.opt()],
                              outs=[gg.opt()]), "gs")
                em.op("gpsimd", lambda e: e.dma_start(
                    out=bubc[:], in_=bu[0:1, :].broadcast_to([NP, E])),
                    "gw", 16)
                for p in range(NEC):
                    em.op("gpsimd", lambda e, p=p: e.dma_start(
                        out=wu_sb[:, E * p:E * (p + 1)],
                        in_=wuT[NP * p:NP * (p + 1), :]), "gw", 16)
                wu_ld = em.val("gw")
                bubc_ld = wu_ld

                em.op("vector", lambda e: e.memset(zP[:, 0:512], 1.0), "vs")
                zp_init = em.val("vs")
                exp_cnt = [0, 0]
                avz_cnt = [0, 0]
                aT_evac = zp_init
                zln_used = 0
                zrep_mm_prev = 0
                uproj_done = 0
                x2_done = 0
                for qt in range(NQTL):
                    em.wait("tensor", "vs", x2_done)
                    for pr in range(NPAIR):
                        def emit_qk(c):
                            buf = c % 2
                            stq = (stqA if buf == 0 else stqB)[:, 0:1024]
                            em.wait("tensor", "ss", exp_cnt[buf])
                            em.op("tensor", lambda e, pr=pr, c=c, qt=qt, stq=stq:
                                  e.matmul(stq[0:NP, 0:512],
                                           kT[0:64, T * pr + NP * c:
                                              T * pr + NP * (c + 1)],
                                           qT[0:64, Q * pr + QT * qt:
                                              Q * pr + QT * (qt + 1)],
                                           start=True, stop=True,
                                           tile_position=(0, 0)), None)
                            em.op("tensor", lambda e, pr=pr, c=c, qt=qt, stq=stq:
                                  e.matmul(stq[0:NP, 512:1024],
                                           kT[64:128, T * pr + NP * c:
                                              T * pr + NP * (c + 1)],
                                           qT[64:128, Q * pr + QT * qt:
                                              Q * pr + QT * (qt + 1)],
                                           start=True, stop=True,
                                           tile_position=(64, 0)), "ts")
                            return em.val("ts")

                        def emit_exp(c, qk_done):
                            buf = c % 2
                            stq = (stqA if buf == 0 else stqB)[:, 0:1024]
                            em.wait("scalar", "ts", qk_done)
                            em.wait("scalar", "ts", avz_cnt[buf])
                            em.op("scalar", lambda e, buf=buf, stq=stq:
                                  e.activation(
                                      ptile[:, 1024 * buf:1024 * (buf + 1)],
                                      stq[:, 0:1024], AF.Exp,
                                      bias=cbexp[:], scale=csexp[:]), "ss")
                            exp_cnt[buf] = em.val("ss")

                        def emit_avz(c):
                            buf = c % 2
                            em.wait("tensor", "ss", exp_cnt[buf])
                            if c == 0:
                                em.wait("tensor", "vs", aT_evac)
                                em.wait("tensor", "ss", zln_used)
                            em.op("tensor", lambda e, pr=pr, c=c, buf=buf:
                                  e.matmul(avP[0:64, 0:512],
                                           vS[:, E * c + 64 * (2 * pr):
                                              E * c + 64 * (2 * pr) + 64],
                                           ptile[:, 1024 * buf:1024 * buf + 512],
                                           start=(c == 0), stop=(c == NTT - 1),
                                           tile_position=(0, 0),
                                           skip_group_check=True), None)
                            em.op("tensor", lambda e, pr=pr, c=c, buf=buf:
                                  e.matmul(avP[64:128, 0:512],
                                           vS[:, E * c + 64 * (2 * pr + 1):
                                              E * c + 64 * (2 * pr + 1) + 64],
                                           ptile[:, 1024 * buf + 512:
                                                 1024 * buf + 1024],
                                           start=(c == 0), stop=(c == NTT - 1),
                                           tile_position=(0, 64),
                                           skip_group_check=True), None)
                            em.op("tensor", lambda e, c=c, buf=buf:
                                  e.matmul(zP[0:1, 0:512],
                                           onesz[:, 0:1],
                                           ptile[:, 1024 * buf:1024 * buf + 512],
                                           start=(c == 0), stop=(c == NTT - 1),
                                           tile_position=(0, 0),
                                           skip_group_check=True), None)
                            em.op("tensor", lambda e, c=c, buf=buf:
                                  e.matmul(zP[32:33, 0:512],
                                           onesz[:, 1:2],
                                           ptile[:, 1024 * buf + 512:
                                                 1024 * buf + 1024],
                                           start=(c == 0), stop=(c == NTT - 1),
                                           tile_position=(0, 32),
                                           skip_group_check=True), "ts")
                            avz_cnt[buf] = em.val("ts")

                        qkd = emit_qk(0)
                        emit_exp(0, qkd)
                        for c in range(1, NTT):
                            qkd = emit_qk(c)
                            emit_avz(c - 1)
                            emit_exp(c, qkd)
                        emit_avz(NTT - 1)
                        em.wait("scalar", "ts", avz_cnt[(NTT - 1) % 2])
                        em.op("scalar", lambda e: e.activation(
                            zln[0:33, :], zP[0:33, 0:512], AF.Ln), "ss")
                        zln_used = em.val("ss")
                        em.wait_now("scalar", "ss")
                        em.wait("scalar", "ts", zrep_mm_prev)
                        em.op("scalar", lambda e: e.activation(
                            zrec[0:33, :], zln[0:33, :], AF.Exp,
                            scale=-1.0), "ss")
                        zr_done = em.val("ss")
                        em.wait("tensor", "ss", zr_done)
                        em.op("tensor", lambda e: e.matmul(
                            zrepP[:, 0:512], sel33[:], zrec[:, :],
                            start=True, stop=True), "ts")
                        zrep_mm = em.val("ts")
                        zrep_mm_prev = zrep_mm
                        em.wait("vector", "ts", zrep_mm)
                        em.op("vector", lambda e: e.tensor_copy(
                            zrs[:], zrepP[:, 0:512]), "vs")
                        em.wait_now("vector", "vs")
                        em.wait("vector", "ts", uproj_done)
                        em.op("vector", lambda e, pr=pr: e.tensor_tensor(
                            aT[:, QT * pr:QT * (pr + 1)],
                            avP[:, 0:512], zrs[:], ALU.mult), "vs")
                        aT_evac = em.val("vs")
                    for mt in range(4):
                        em.wait("sync", "vs", x2_done)
                        em.wait("sync", "sx2", em.val("sx2"))
                        em.op("sync", lambda e, qt=qt, mt=mt: e.dma_start(
                            out=xr_sb[:],
                            in_=xb[QT * qt + NP * mt:QT * qt + NP * (mt + 1), :]),
                            "lxr", 16)
                        lxr = em.val("lxr")
                        em.wait("vector", "lxr", lxr)
                        em.wait("vector", "gw", bubc_ld)
                        em.op("vector", lambda e: e.tensor_add(
                            xrb_sb[:], xr_sb[:], bubc[:]), "vs")
                        xrb_done = em.val("vs")
                        em.wait("tensor", "ss", exp_cnt[(NTT - 1) % 2])
                        em.wait("tensor", "vs", aT_evac)
                        em.wait("tensor", "vs", x2_done)
                        em.wait("tensor", "gw", wu_ld)
                        for n in range(2):
                            for p in range(NEC):
                                em.op("tensor", lambda e, p=p, n=n, mt=mt:
                                      e.matmul(
                                          stqA[:, 512 * n:512 * (n + 1)],
                                          aT[:, QT * p + NP * mt:
                                             QT * p + NP * (mt + 1)],
                                          wu_sb[:, E * p + 512 * n:
                                                E * p + 512 * (n + 1)],
                                          start=(p == 0), stop=(p == NEC - 1)),
                                      "ts" if (p == NEC - 1 and n == 1) else None)
                        uproj_done = em.val("ts")
                        em.wait("vector", "ts", uproj_done)
                        em.wait("vector", "vs", xrb_done)
                        em.wait("vector", "sx2", em.val("sx2"))
                        em.op("vector", lambda e: e.tensor_tensor(
                            x2_sb[:], stqA[:, 0:1024], xrb_sb[:],
                            ALU.add), "vs")
                        x2v = em.val("vs")
                        em.wait("sync", "vs", x2v)
                        em.op("sync", lambda e, qt=qt, mt=mt: e.dma_start(
                            out=x2s[QT * qt + NP * mt:QT * qt + NP * (mt + 1), :],
                            in_=x2_sb[:]), "sx2", 16)
                        x2_done = x2v
                flush_block(nc, em, sems)

        with ExitStack() as esF:
            wgq_sb = sb(esF, "wgq_sb", [NP, NEC * I], BF16)
            wuq_sb = sb(esF, "wuq_sb", [NP, NEC * I], BF16)
            wdq_sb = sb(esF, "wdq_sb", [NP, NIC * E], BF16)
            xf = sb(esF, "xf", [NP, 3 * E])
            xq = sb(esF, "xq", [NP, 2 * E], BF16)
            xqT = sb(esF, "xqT", [NP, E], BF16)
            gsb = sb(esF, "gsb", [NP, 512])
            hbuf = sb(esF, "hbuf", [NP, I])
            hq = sb(esF, "hq", [NP, I], BF16)
            hqT = sb(esF, "hqT", [NP, I], BF16)
            ysb = sb(esF, "ysb", [NP, E])
            qB = sb(esF, "fqB", [NP, 1408])
            qP = sb(esF, "fqP", [NP, 1408])
            qQ = sb(esF, "fqQ", [NP, 1408])
            qR = sb(esF, "fqR", [NP, 1408])
            qsgn = sb(esF, "fqsgn", [NP, 1408], BF16)
            qam = sb(esF, "fqam", [NP, 88])
            xl2 = qP[:, 0:E]
            qs = sb(esF, "fqs", [NP, 88])
            qr = sb(esF, "fqr", [NP, 88])

            def emit_wloads():
                em.wait_now("scalar", "gs")
                for p in range(NEC):
                    em.op("scalar", lambda e, p=p: e.dma_start(
                        out=wgq_sb[:, I * p:I * (p + 1)]
                        .rearrange("p2 (t c) -> p2 t c", c=NP),
                        in_=wgg[0:NIC, NP * p:NP * (p + 1), :]
                        .transpose([1, 0, 2])), "lwq", 16)
                    em.op("scalar", lambda e, p=p: e.dma_start(
                        out=wuq_sb[:, I * p:I * (p + 1)]
                        .rearrange("p2 (t c) -> p2 t c", c=NP),
                        in_=wug[0:NIC, NP * p:NP * (p + 1), :]
                        .transpose([1, 0, 2])), "lwq", 16)
                wgu = em.val("lwq")
                for j in range(NIC):
                    em.op("scalar", lambda e, j=j: e.dma_start(
                        out=wdq_sb[:, E * j:E * (j + 1)]
                        .rearrange("p2 (t c) -> p2 t c", c=NP),
                        in_=wdg[0:NEC, NP * j:NP * (j + 1), :]
                        .transpose([1, 0, 2])), "lwq", 16)
                return wgu, em.val("lwq")

            def pe_warm():
                em.wait_now("tensor", "vs")
                em.op("tensor", lambda e: e.matmul(
                    zrepP[0:64, 0:64], identb[:, 0:64], identb[:, 0:64],
                    start=True, stop=True), None)

            NW = [512, 512, 512, 512, 512, 256]
            xf_used = [0, 0, 0]
            gu_used = [0, 0]
            tp_evac = 0
            y_done = 0
            hTT_prev = 0
            pend_y = None
            lxs = ["lx0", "lx1"]
            xq_read = [0, 0]
            xq_done = [0, 0]

            ld_pf = {}

            def stage1a_load(t):
                slot = t % 3
                em.wait("sync", "vs", xf_used[slot])
                em.op("sync", lambda e, t=t, slot=slot: e.dma_start(
                    out=xf[:, E * slot:E * (slot + 1)],
                    in_=x2s[NP * t:NP * (t + 1), :]), lxs[t % 2], 16)
                ld_pf[t] = em.val(lxs[t % 2])

            def stage1a_stats(t):
                slot = t % 3
                em.wait("vector", lxs[t % 2], ld_pf[t])
                em.wait("scalar", lxs[t % 2], ld_pf[t])
                return emit_ln_stats(em, xf[:, E * slot:E * (slot + 1)],
                                     qB, lnm, lnv, lnr1, lnr2, ceps)

            def stage1b(t, sdone):
                slot = t % 3
                emit_ln_apply(em, xf[:, E * slot:E * (slot + 1)], xl2,
                              lnm, lnr2, sdone)
                qslot = t % 2
                em.wait("vector", "ts", xq_read[qslot])
                emit_quant(em, xl2, xq[:, E * qslot:E * (qslot + 1)],
                           qB, qP, qQ, qR, qsgn, qam, qs, qr, E,
                           x_alias_p=True, warm_cb=pe_warm)
                xq_done[qslot] = em.val("vs")

            def emit_gu(n):
                w = NW[n]
                c0 = 512 * n
                ps = n % 2
                psT = stqA if ps == 0 else stqB
                gP = psT[:, 0:512]
                uP = psT[:, 512:1024]
                em.wait("tensor", "ss", xqT_done)
                em.wait("tensor", "vs", gu_used[ps])
                for k in range(NEC):
                    em.op("tensor", lambda e, k=k, c0=c0, w=w, gP=gP:
                          e.matmul(gP[:, 0:w],
                                   xqT[:, NP * k:NP * (k + 1)],
                                   wgq_sb[:, I * k + c0:I * k + c0 + w],
                                   start=(k == 0), stop=(k == NEC - 1)),
                          None)
                for k in range(NEC):
                    em.op("tensor", lambda e, k=k, c0=c0, w=w, uP=uP:
                          e.matmul(uP[:, 0:w],
                                   xqT[:, NP * k:NP * (k + 1)],
                                   wuq_sb[:, I * k + c0:I * k + c0 + w],
                                   start=(k == 0), stop=(k == NEC - 1)),
                          "ts" if k == NEC - 1 else None)
                gu_dones[n] = em.val("ts")

            def emit_h(n):
                nonlocal hTT_prev
                w = NW[n]
                c0 = 512 * n
                ps = n % 2
                psT = stqA if ps == 0 else stqB
                gP = psT[:, 0:512]
                uP = psT[:, 512:1024]
                em.wait("scalar", "ts", gu_dones[n])
                em.wait("scalar", "vs", hTT_prev)
                em.op("scalar", lambda e, w=w, gP=gP: e.activation(
                    gsb[:, 0:w], gP[:, 0:w], AF.Silu), "ss")
                sil = em.val("ss")
                em.wait("vector", "ss", sil)
                em.op("vector", lambda e, c0=c0, w=w, uP=uP:
                      e.tensor_tensor(hbuf[:, c0:c0 + w], gsb[:, 0:w],
                                      uP[:, 0:w], ALU.mult), "vs")
                gu_used[ps] = em.val("vs")
                hTT_prev = gu_used[ps]

            def flush_hd(pt):
                nonlocal tp_evac, dn_done, y_done
                for j0 in range(0, NIC, 8):
                    jn = min(8, NIC - j0)
                    em.wait("tensor", "vs", hq_done)
                    em.wait("tensor", "ss", tp_evac)
                    for jj in range(jn):
                        j = j0 + jj
                        em.op("tensor", lambda e, j=j, jj=jj: e.transpose(
                            tpP[:, NP * jj:NP * (jj + 1)],
                            hq[:, NP * j:NP * (j + 1)], identb[:]), "ts")
                    tdone = em.val("ts")
                    em.wait("scalar", "ts", tdone)
                    em.op("scalar", lambda e, j0=j0, jn=jn: e.activation(
                        hqT[:, NP * j0:NP * (j0 + jn)],
                        tpP[:, 0:NP * jn], AF.Copy), "ss")
                    tp_evac = em.val("ss")
                hqT_done = tp_evac
                em.wait("tensor", "ss", hqT_done)
                em.wait("tensor", "vs", y_done)
                em.wait("tensor", "lwq", wq_ld)
                for j in range(NIC):
                    for n in range(2):
                        em.op("tensor", lambda e, j=j, n=n: e.matmul(
                            (avP if n == 0 else zP)[:, 0:512],
                            hqT[:, NP * j:NP * (j + 1)],
                            wdq_sb[:, E * j + 512 * n:E * j + 512 * (n + 1)],
                            start=(j == 0), stop=(j == NIC - 1)),
                            "ts" if (j == NIC - 1 and n == 1) else None)
                dn_done = em.val("ts")

            def flush_y(pt, pslot):
                nonlocal y_done
                em.wait("vector", "ts", dn_done)
                em.wait("vector", "sy", em.val("sy"))
                em.op("vector", lambda e, pslot=pslot: e.tensor_tensor(
                    ysb[:, 0:512], avP[:, 0:512],
                    xf[:, E * pslot:E * pslot + 512], ALU.add), "vs")
                em.wait_now("vector", "vs")
                em.op("vector", lambda e, pslot=pslot: e.tensor_tensor(
                    ysb[:, 512:1024], zP[:, 0:512],
                    xf[:, E * pslot + 512:E * pslot + 1024], ALU.add), "vs")
                y_done = em.val("vs")
                xf_used[pslot] = y_done
                em.wait("sync", "vs", y_done)
                em.op("sync", lambda e, pt=pt: e.dma_start(
                    out=y[NP * pt:NP * (pt + 1), :], in_=ysb[:]), "sy", 16)

            dn_done = 0
            hq_done = 0
            hqtp_done = 0
            pend = None
            em.wait_now("sync", "sx2")
            stage1a_load(0)
            stage1a_load(1)
            sd_next = stage1a_stats(0)
            stage1b(0, sd_next)
            wgu_ld, wq_ld = emit_wloads()
            for tt in range(NQT):
                slot = tt % 3
                qslot = tt % 2
                em.wait("tensor", "vs", xq_done[qslot])
                em.wait("tensor", "ss", tp_evac)
                for p in range(NEC):
                    em.op("tensor", lambda e, p=p, qslot=qslot: e.transpose(
                        tpP[:, NP * p:NP * (p + 1)],
                        xq[:, E * qslot + NP * p:E * qslot + NP * (p + 1)],
                        identb[:]), "ts")
                tdone = em.val("ts")
                xq_read[qslot] = tdone
                em.wait("scalar", "ts", tdone)
                em.op("scalar", lambda e: e.activation(
                    xqT[:], tpP[:, 0:E], AF.Copy), "ss")
                tp_evac = em.val("ss")
                xqT_done = tp_evac
                if tt + 1 < NQT:
                    sd_next = stage1a_stats(tt + 1)
                if pend is not None:
                    flush_hd(pend[0])
                    hqtp_done = em.val("ts")
                em.wait("tensor", "lwq", wgu_ld)
                gu_dones = {}
                emit_gu(0)
                emit_gu(1)
                if tt + 1 < NQT:
                    stage1b(tt + 1, sd_next)
                if pend is not None:
                    flush_y(pend[0], pend[1])
                    pend = None
                if tt + 2 < NQT:
                    stage1a_load(tt + 2)
                for n in range(2, 8):
                    emit_h(n - 2)
                    if n < 6:
                        emit_gu(n)
                em.wait("vector", "ts", hqtp_done)
                emit_quant(em, hbuf[:, 0:1408], hq[:, 0:1408],
                           qB, qP, qQ, qR, qsgn, qam, qs, qr, 1408,
                           warm_cb=pe_warm)
                emit_quant(em, hbuf[:, 1408:2816], hq[:, 1408:2816],
                           qB, qP, qQ, qR, qsgn, qam, qs, qr, 1408,
                           warm_cb=pe_warm)
                hq_done = em.val("vs")
                pend = (tt, slot)
            flush_hd(pend[0])
            flush_y(pend[0], pend[1])
            em.wait_now("sync", "sy")
            flush_block(nc, em, sems)

        nc.finalize()
    return nc


_NC_CACHE = None


def kernel(L0_attn_x, L0_attn_Wk, L0_attn_Wq, L0_attn_Wv, L0_attn_Wu,
           L0_attn_bu, L0_ffn_w_gate, L0_ffn_w_up, L0_ffn_w_down):
    global _NC_CACHE
    x = np.asarray(L0_attn_x, dtype=np.float32)
    wkT = np.ascontiguousarray(np.asarray(L0_attn_Wk, np.float32).T)
    wqT = np.ascontiguousarray(np.asarray(L0_attn_Wq, np.float32).T)
    wvT = np.ascontiguousarray(np.asarray(L0_attn_Wv, np.float32).T)
    wuT = np.ascontiguousarray(np.asarray(L0_attn_Wu, np.float32).T)
    bu = np.ascontiguousarray(np.asarray(L0_attn_bu, np.float32)[None, :])
    wg = np.ascontiguousarray(np.asarray(L0_ffn_w_gate, np.float32))
    wu2 = np.ascontiguousarray(np.asarray(L0_ffn_w_up, np.float32))
    wd = np.ascontiguousarray(np.asarray(L0_ffn_w_down, np.float32))

    if _NC_CACHE is None:
        _NC_CACHE = build_nc()
    nc = _NC_CACHE

    in_maps = []
    for c in range(8):
        b, h = c // 2, c % 2
        xbat = x[b]
        xrot = np.ascontiguousarray(
            np.concatenate([xbat[Q * h:Q * (h + 1)], xbat[Q * (1 - h):Q * (2 - h)]],
                           axis=0))
        gt = [min(3 * c + i, 21) for i in range(3)]
        wgs = np.ascontiguousarray(
            np.concatenate([wg[NP * t:NP * (t + 1)] for t in gt], axis=0))
        wus = np.ascontiguousarray(
            np.concatenate([wu2[NP * t:NP * (t + 1)] for t in gt], axis=0))
        td = min(c, 7)
        wds = np.ascontiguousarray(wd[NP * td:NP * (td + 1)])
        in_maps.append({"xb": xrot, "wkT": wkT, "wqT": wqT, "wvT": wvT,
                        "wuT": wuT, "bu": bu, "wg": wg, "wu2": wu2, "wd": wd,
                        "wgs": wgs, "wus": wus, "wds": wds})
    res = run_bass_kernel_spmd(nc, in_maps, core_ids=list(range(8)))
    out = np.empty((4, T, E), dtype=np.float32)
    for c in range(8):
        b, h = c // 2, c % 2
        out[b, Q * h:Q * (h + 1), :] = res.results[c]["y"]
    return out


# version 27
# speedup vs baseline: 1.0328x; 1.0328x over previous
import numpy as np
from contextlib import ExitStack
import concourse.bass as bass
import concourse.mybir as mybir
import concourse.bass_utils as _bu
from concourse.bass_utils import run_bass_kernel_spmd

if not getattr(_bu.get_walrus_args, "_ldw_patched", False):
    _orig_gwa = _bu.get_walrus_args

    def _gwa_ldw(*a, **k):
        return [x.replace("--enable-ldw-opt=false", "--enable-ldw-opt=true")
                if isinstance(x, str) else x for x in _orig_gwa(*a, **k)]

    _gwa_ldw._ldw_patched = True
    _bu.get_walrus_args = _gwa_ldw

F32 = mybir.dt.float32
BF16 = mybir.dt.bfloat16
AF = mybir.ActivationFunctionType
ALU = mybir.AluOpType
AX = mybir.AxisListType

T = 4096
Q = 2048
E = 1024
NH = 16
S = 64
I = 2816
NP = 128
NEC = E // NP
NIC = I // NP
NTT = T // NP
NQT = Q // NP
QT = 512
NQTL = Q // QT
NPAIR = NH // 2

ENGS = ("sync", "gpsimd", "vector", "scalar", "tensor")


class Emit:

    def __init__(self):
        self.ops = {e: [] for e in ENGS}
        self.cnt = {}

    def op(self, eng, fn, inc=None, inc_n=1):
        self.ops[eng].append((fn, inc, inc_n))
        if inc is not None:
            self.cnt[inc] = self.cnt.get(inc, 0) + inc_n
        return self.cnt.get(inc, 0)

    def wait(self, eng, sem, val):
        if val > 0:
            self.ops[eng].append((("W", sem, val), None, 0))

    def wait_now(self, eng, sem):
        self.wait(eng, sem, self.cnt.get(sem, 0))

    def val(self, sem):
        return self.cnt.get(sem, 0)

def flush_block(nc, em, sems):
    ops = em.ops
    em.ops = {e: [] for e in ENGS}
    with nc.Block() as blk:
        def runner(lst):
            def run(e):
                for fn, inc, inc_n in lst:
                    if isinstance(fn, tuple) and fn[0] == "W":
                        e.wait_ge(sems[fn[1]], fn[2])
                    else:
                        ins = fn(e)
                        if inc is not None:
                            ins.then_inc(sems[inc], inc_n)
            return run
        if ops["sync"]:
            blk.sync(runner(ops["sync"]))
        if ops["gpsimd"]:
            blk.gpsimd(runner(ops["gpsimd"]))
        if ops["vector"]:
            blk.vector(runner(ops["vector"]))
        if ops["scalar"]:
            blk.scalar(runner(ops["scalar"]))
        if ops["tensor"]:
            blk.tensor(runner(ops["tensor"]))


def emit_quant(em, x_ap, out_ap, B, P, Qb, R, sgn, amax, s_, r_, N,
               x_alias_p=False, warm_cb=None):
    nb = N // 16
    x3 = x_ap.rearrange("p (b k) -> p b k", k=16)
    rb = s_[:, 0:nb].unsqueeze(2).broadcast_to([128, nb, 16])
    qb = r_[:, 0:nb].unsqueeze(2).broadcast_to([128, nb, 16])

    nv = [0]

    def V(fn):
        return em.op("vector", fn, "vs")

    def VW():
        em.wait_now("vector", "vs")
        nv[0] += 1
        if warm_cb is not None and nv[0] % 2 == 0:
            warm_cb()

    em.wait_now("scalar", "vs")
    em.op("scalar", lambda e: e.activation(B[:, 0:N], x_ap, AF.Abs), "ss")
    sc_abs = em.val("ss")
    em.op("scalar", lambda e: e.activation(sgn[:, 0:N], x_ap, AF.Sign), "ss")
    sc_sgn = em.val("ss")
    V(lambda e: e.tensor_reduce(amax[:, 0:nb], x3, AX.X, ALU.max,
                                apply_absolute_value=True)); VW()
    V(lambda e: e.reciprocal(r_[:, 0:nb], amax[:, 0:nb])); VW()
    V(lambda e: e.tensor_scalar(s_[:, 0:nb], r_[:, 0:nb], 6.0, 1e12,
                                ALU.mult, ALU.min)); VW()
    V(lambda e: e.reciprocal(r_[:, 0:nb], s_[:, 0:nb])); VW()
    em.wait("vector", "ss", sc_sgn if x_alias_p else sc_abs)
    V(lambda e: e.tensor_tensor(
        P[:, 0:N].rearrange("p (b k) -> p b k", k=16),
        B[:, 0:N].rearrange("p (b k) -> p b k", k=16), rb, ALU.mult)); VW()
    V(lambda e: e.tensor_single_scalar(Qb[:, 0:N], P[:, 0:N], 0.25, ALU.is_gt)); VW()
    V(lambda e: e.scalar_tensor_tensor(R[:, 0:N], P[:, 0:N], 0.75, Qb[:, 0:N],
                                       ALU.is_gt, ALU.add)); VW()
    V(lambda e: e.scalar_tensor_tensor(Qb[:, 0:N], P[:, 0:N], 1.25, R[:, 0:N],
                                       ALU.is_gt, ALU.add)); VW()
    V(lambda e: e.scalar_tensor_tensor(R[:, 0:N], P[:, 0:N], 1.75, Qb[:, 0:N],
                                       ALU.is_gt, ALU.add)); VW()
    V(lambda e: e.tensor_single_scalar(Qb[:, 0:N], P[:, 0:N], 2.5, ALU.is_gt)); VW()
    V(lambda e: e.scalar_tensor_tensor(B[:, 0:N], P[:, 0:N], 3.5, Qb[:, 0:N],
                                       ALU.is_gt, ALU.add)); VW()
    V(lambda e: e.scalar_tensor_tensor(Qb[:, 0:N], P[:, 0:N], 5.0, B[:, 0:N],
                                       ALU.is_gt, ALU.add)); VW()
    V(lambda e: e.scalar_tensor_tensor(B[:, 0:N], P[:, 0:N], 5.0, Qb[:, 0:N],
                                       ALU.is_gt, ALU.add)); VW()
    V(lambda e: e.scalar_tensor_tensor(P[:, 0:N], R[:, 0:N], 0.5, B[:, 0:N],
                                       ALU.mult, ALU.add)); VW()
    em.wait("vector", "ss", sc_sgn)
    V(lambda e: e.tensor_tensor(Qb[:, 0:N], P[:, 0:N], sgn[:, 0:N], ALU.mult)); VW()
    V(lambda e: e.tensor_tensor(
        out_ap.rearrange("p (b k) -> p b k", k=16),
        Qb[:, 0:N].rearrange("p (b k) -> p b k", k=16), qb, ALU.mult)); VW()


def emit_ln_stats(em, x_ap, sq_scr, lnm, lnv, lnr1, lnr2, ceps, n=E):
    em.op("vector", lambda e: e.reduce_sum(lnv[:], x_ap, AX.X), "vs")
    em.wait_now("vector", "vs")
    em.op("vector", lambda e: e.tensor_scalar_mul(lnm[:], lnv[:], 1.0 / n), "vs")
    em.wait_now("scalar", "vs")
    em.op("scalar", lambda e: e.activation(sq_scr[:, 0:n], x_ap, AF.Square,
                                           bias=lnm[:], scale=-1.0,
                                           accum_out=lnv[:]), "ss")
    em.wait_now("scalar", "ss")
    em.op("scalar", lambda e: e.activation(lnr1[:], lnv[:], AF.Ln,
                                           bias=ceps[:], scale=1.0 / n), "ss")
    em.wait_now("scalar", "ss")
    em.op("scalar", lambda e: e.activation(lnr2[:], lnr1[:], AF.Exp,
                                           scale=-0.5), "ss")
    return em.val("ss")


def emit_ln_apply(em, x_ap, out_ap, lnm, lnr2, ss_done):
    em.wait("vector", "ss", ss_done)
    em.op("vector", lambda e: e.tensor_scalar(out_ap, x_ap, lnm[:], lnr2[:],
                                              ALU.subtract, ALU.mult), "vs")
    em.wait_now("vector", "vs")


def emit_ln(em, x_ap, out_ap, sq_scr, lnm, lnv, lnr1, lnr2, ceps, n=E):
    em.op("vector", lambda e: e.reduce_sum(lnv[:], x_ap, AX.X), "vs")
    em.wait_now("vector", "vs")
    em.op("vector", lambda e: e.tensor_scalar_mul(lnm[:], lnv[:], 1.0 / n), "vs")
    em.wait_now("vector", "vs")
    em.wait_now("scalar", "vs")
    em.op("scalar", lambda e: e.activation(sq_scr[:, 0:n], x_ap, AF.Square,
                                           bias=lnm[:], scale=-1.0,
                                           accum_out=lnv[:]), "ss")
    em.wait_now("scalar", "ss")
    em.op("scalar", lambda e: e.activation(lnr1[:], lnv[:], AF.Ln,
                                           bias=ceps[:], scale=1.0 / n), "ss")
    em.wait_now("scalar", "ss")
    em.op("scalar", lambda e: e.activation(lnr2[:], lnr1[:], AF.Exp,
                                           scale=-0.5), "ss")
    em.wait_now("vector", "ss")
    em.op("vector", lambda e: e.tensor_scalar(out_ap, x_ap, lnm[:], lnr2[:],
                                              ALU.subtract, ALU.mult), "vs")
    em.wait_now("vector", "vs")


def build_nc():
    nc = bass.Bass("TRN2", target_bir_lowering=False,
                   detect_race_conditions=False)
    xb = nc.dram_tensor("xb", [T, E], F32, kind="ExternalInput").ap()
    wkT = nc.dram_tensor("wkT", [E, E], F32, kind="ExternalInput").ap()
    wqT = nc.dram_tensor("wqT", [E, E], F32, kind="ExternalInput").ap()
    wvT = nc.dram_tensor("wvT", [E, E], F32, kind="ExternalInput").ap()
    wuT = nc.dram_tensor("wuT", [E, E], F32, kind="ExternalInput").ap()
    bu = nc.dram_tensor("bu", [1, E], F32, kind="ExternalInput").ap()
    wg = nc.dram_tensor("wg", [I, E], F32, kind="ExternalInput").ap()
    wu2 = nc.dram_tensor("wu2", [I, E], F32, kind="ExternalInput").ap()
    wd = nc.dram_tensor("wd", [E, I], F32, kind="ExternalInput").ap()
    wgs = nc.dram_tensor("wgs", [3 * NP, E], F32, kind="ExternalInput").ap()
    wus = nc.dram_tensor("wus", [3 * NP, E], F32, kind="ExternalInput").ap()
    wds = nc.dram_tensor("wds", [1 * NP, I], F32, kind="ExternalInput").ap()
    y = nc.dram_tensor("y", [Q, E], F32, kind="ExternalOutput").ap()
    x2s = nc.dram_tensor("x2s", [Q, E], F32).ap()
    xls = nc.dram_tensor("xls", [NEC, NP, T], BF16).ap()
    wg_loc = nc.dram_tensor("wg_loc", [3, E, NP], BF16).ap()
    wu_loc = nc.dram_tensor("wu_loc", [3, E, NP], BF16).ap()
    wd_loc = nc.dram_tensor("wd_loc", [1, I, NP], BF16).ap()
    wgg = nc.dram_tensor("wgg", [24, E, NP], BF16).ap()
    wug = nc.dram_tensor("wug", [24, E, NP], BF16).ap()
    wdg = nc.dram_tensor("wdg", [8, I, NP], BF16).ap()

    em = Emit()
    es = ExitStack()
    sems = {}

    def sem(name):
        sems[name] = es.enter_context(nc.semaphore(name))

    def sb(scope, name, shape, dt=F32):
        return scope.enter_context(nc.sbuf_tensor(name, shape, dt))

    with es:
        for s in ("lx0", "lx1", "gw", "lw", "lxr", "sx2", "sy",
                  "vs", "ss", "ts", "gs", "sw", "lwq", "gww"):
            sem(s)
        stqA = es.enter_context(nc.psum_tensor("stqA", [NP, 1024], F32))
        stqB = es.enter_context(nc.psum_tensor("stqB", [NP, 1024], F32))
        avP = es.enter_context(nc.psum_tensor("avP", [NP, 512], F32))
        zP = es.enter_context(nc.psum_tensor("zP", [NP, 512], F32))
        zrepP = es.enter_context(nc.psum_tensor("zrepP", [NP, 512], F32))
        tpP = es.enter_context(nc.psum_tensor("tpP", [NP, 1024], BF16))
        identb = sb(es, "identb", [NP, NP], BF16)
        onestmp = sb(es, "onestmp", [NP, NP], BF16)
        onesz = sb(es, "onesz", [NP, 2], BF16)
        sel33 = sb(es, "sel33", [33, NP])
        cbexp = sb(es, "cbexp", [NP, 1])
        csexp = sb(es, "csexp", [NP, 1])
        ceps = sb(es, "ceps", [NP, 1])
        lnm = sb(es, "lnm", [NP, 1])
        lnv = sb(es, "lnv", [NP, 1])
        lnr1 = sb(es, "lnr1", [NP, 1])
        lnr2 = sb(es, "lnr2", [NP, 1])
        lnmB = sb(es, "lnmB", [NP, 1])
        lnvB = sb(es, "lnvB", [NP, 1])
        lnr1B = sb(es, "lnr1B", [NP, 1])
        lnr2B = sb(es, "lnr2B", [NP, 1])

        def G(fn):
            return em.op("gpsimd", fn, "gs")

        G(lambda e: e.memset(onestmp[:], 1.0))
        G(lambda e: e.memset(onesz[:], 1.0))
        G(lambda e: e.memset(sel33[:], 0.0))
        G(lambda e: e.memset(cbexp[:], -16.0))
        G(lambda e: e.memset(csexp[:], 0.125))
        G(lambda e: e.memset(ceps[:], 1e-5))
        em.wait_now("gpsimd", "gs")
        G(lambda e: e.affine_select(identb[:], onestmp[:], [[1, NP]],
                                    ALU.is_equal, 0.0, channel_multiplier=-1))
        G(lambda e: e.memset(sel33[0:1, 0:64], 1.0))
        G(lambda e: e.memset(sel33[32:33, 64:128], 1.0))
        em.wait_now("gpsimd", "gs")
        flush_block(nc, em, sems)

        with ExitStack() as esPA:
            kT = sb(esPA, "kT", [NP, NEC * T], BF16)
            qT = sb(esPA, "qT", [NP, NEC * Q], BF16)
            vS = None

            def emit_proj_phase(which):
                with ExitStack() as esP:
                    if which == "kq":
                        wk_sb = sb(esP, "wk_sb", [NP, NEC * E], BF16)
                        wq_sb = sb(esP, "wq_sb", [NP, NEC * E], BF16)
                        win = sb(esP, "win", [NP, I])
                        wqbf = sb(esP, "wqbf", [NP, I], BF16)
                        wstage = sb(esP, "wstage", [NP, I], BF16)
                        qB = sb(esP, "qB", [NP, 1408])
                        qP = sb(esP, "qP", [NP, 1408])
                        qQ = sb(esP, "qQ", [NP, 1408])
                        qR = sb(esP, "qR", [NP, 1408])
                        qsgn = sb(esP, "qsgn", [NP, 1408], BF16)
                        qam = sb(esP, "qam", [NP, 88])
                        qs = sb(esP, "qs", [NP, 88])
                        qr = sb(esP, "qr", [NP, 88])
                        wjobs = ([(wgs, wg_loc, t, 3, E) for t in range(3)]
                                 + [(wus, wu_loc, t, 3, E) for t in range(3)]
                                 + [(wds, wd_loc, t, 1, I) for t in range(1)])
                        wstate = {"win": [0, 0], "stage": [0, 0]}

                        def emit_wtile(idx, tpst):
                            srcw, dstw, t, ntile, N = wjobs[idx]
                            slot = 0
                            em.wait("scalar", "vs", wstate["win"][slot])
                            em.op("scalar", lambda e, srcw=srcw, t=t, N=N,
                                  slot=slot: e.dma_start(
                                      out=win[:, 0:N],
                                      in_=srcw[NP * t:NP * (t + 1), 0:N]),
                                  "lwq", 16)
                            ld = em.val("lwq")
                            em.wait("vector", "lwq", ld)
                            em.wait("scalar", "lwq", ld)
                            nch = (N + 1407) // 1408
                            win_free = None
                            for ci in range(nch):
                                c0 = ci * 1408
                                w = min(1408, N - c0)
                                vs0 = em.val("vs")
                                emit_quant(em,
                                           win[:, c0:c0 + w],
                                           wqbf[:, c0:c0 + w],
                                           qB, qP, qQ, qR, qsgn,
                                           qam, qs, qr, w)
                                win_free = vs0 + 1
                            wstate["win"][slot] = win_free
                            qv = em.val("vs")
                            nct = N // NP
                            for j0 in range(0, nct, 8):
                                jn = min(8, nct - j0)
                                em.wait("tensor", "vs", qv)
                                em.wait("tensor", "vs", tpst["evac"])
                                if j0 == 0:
                                    em.wait("tensor", "gww",
                                            wstate["stage"][0])
                                for jj in range(jn):
                                    j = j0 + jj
                                    em.op("tensor", lambda e, j=j, jj=jj:
                                          e.transpose(
                                              tpP[:, NP * jj:NP * (jj + 1)],
                                              wqbf[:, NP * j:NP * (j + 1)],
                                              identb[:]), "ts")
                                tdone = em.val("ts")
                                em.wait("vector", "ts", tdone)
                                em.wait("vector", "gww", wstate["stage"][0])
                                em.op("vector", lambda e, j0=j0, jn=jn:
                                      e.tensor_copy(
                                          wstage[:, NP * j0:NP * (j0 + jn)],
                                          tpP[:, 0:NP * jn]), "vs")
                                tpst["evac"] = em.val("vs")
                            em.wait("gpsimd", "vs", tpst["evac"])
                            em.op("gpsimd", lambda e, dstw=dstw, t=t, N=N:
                                  e.dma_start(
                                      out=dstw[t, 0:N, :]
                                      .rearrange("(j p) c -> p j c", p=NP),
                                      in_=wstage[:, 0:N]
                                      .rearrange("p (j c) -> p j c", c=NP)),
                                  "gww", 16)
                            wstate["stage"][0] = em.val("gww")
                    else:
                        wv_sb = sb(esP, "wv_sb", [NP, NEC * E], BF16)
                    xin = sb(esP, "xin_" + which, [NP, 2 * E])
                    xlbf = sb(esP, "xlbf_" + which, [NP, E], BF16)
                    xlT = sb(esP, "xlT_" + which, [NP, 4 * E], BF16)
                    lnSq = sb(esP, "lnSq_" + which, [NP, E])
                    if which == "kq":
                        for p in range(NEC):
                            em.op("gpsimd", lambda e, p=p: e.dma_start(
                                out=wk_sb[:, E * p:E * (p + 1)],
                                in_=wkT[NP * p:NP * (p + 1), :]), "gw", 16)
                            em.op("gpsimd", lambda e, p=p: e.dma_start(
                                out=wq_sb[:, E * p:E * (p + 1)],
                                in_=wqT[NP * p:NP * (p + 1), :]), "gw", 16)
                        em.wait_now("gpsimd", "gw")
                        for p in range(NEC):
                            dg = slice(E * p + NP * p, E * p + NP * (p + 1))
                            for wsb, cm, pat in ((wk_sb, 1, -1),
                                                 (wq_sb, -1, 1)):
                                em.op("gpsimd", lambda e, wsb=wsb, dg=dg:
                                      e.tensor_copy(onestmp[:], wsb[:, dg]),
                                      "gs")
                                em.wait_now("gpsimd", "gs")
                                em.op("gpsimd", lambda e, wsb=wsb, dg=dg,
                                      cm=cm, pat=pat: e.affine_select(
                                          wsb[:, dg], onestmp[:],
                                          [[pat, NP]], ALU.is_ge, 0.0,
                                          channel_multiplier=cm), "gs")
                                em.wait_now("gpsimd", "gs")
                        wmask_done = em.val("gs")
                        em.wait("tensor", "gs", wmask_done)
                    else:
                        for p in range(NEC):
                            em.op("gpsimd", lambda e, p=p: e.dma_start(
                                out=wv_sb[:, E * p:E * (p + 1)],
                                in_=wvT[NP * p:NP * (p + 1), :]), "gw", 16)
                        em.wait("tensor", "gw", em.val("gw"))
                    kq_evac = [0, 0]
                    q_evac = [0, 0]
                    v_evac = [0, 0]
                    tpst = {"evac": 0}
                    widx = [0]
                    xlT_used = [0, 0]

                    def do_apply(i, slot, pset, sdone):
                        emit_ln_apply(em, xin[:, E * slot:E * (slot + 1)],
                                      xlbf[:, 0:E], pset[0], pset[3], sdone)
                        xin_used[slot] = em.val("vs")
                        em.wait_now("tensor", "vs")
                        em.wait("tensor", "vs", tpst["evac"])
                        for p in range(NEC):
                            em.op("tensor", lambda e, p=p: e.transpose(
                                tpP[:, NP * p:NP * (p + 1)],
                                xlbf[:, NP * p:NP * (p + 1)],
                                identb[:]), "ts")
                        tdone = em.val("ts")
                        em.wait("vector", "ts", tdone)
                        em.wait("vector", "ts", xlT_used[0])
                        em.op("vector", lambda e, i=i: e.tensor_copy(
                            xlT[:, 0:4 * E]
                            .rearrange("p (c q) -> p c q", q=512)
                            [:, :, NP * i:NP * (i + 1)],
                            tpP[:].rearrange("p (c q) -> p c q", q=NP)),
                            "vs")
                        tpst["evac"] = em.val("vs")
                    lxs = ["lx0", "lx1"]
                    xin_used = [0, 0]
                    for g in range(NEC):
                        gs_ = 0
                        xo = 0
                        pend = None
                        if which == "kq":
                            em.wait("vector", "gww", em.val("gww"))
                        for i in range(4):
                            slot = i % 2
                            pset = ((lnm, lnv, lnr1, lnr2) if i % 2 == 0
                                    else (lnmB, lnvB, lnr1B, lnr2B))
                            em.wait("sync", "vs", xin_used[slot])
                            em.op("sync", lambda e, g=g, i=i, slot=slot:
                                  e.dma_start(
                                      out=xin[:, E * slot:E * (slot + 1)],
                                      in_=xb[512 * g + NP * i:
                                             512 * g + NP * (i + 1), :]),
                                  lxs[slot], 16)
                            ldv = em.val(lxs[slot])
                            em.wait("vector", lxs[slot], ldv)
                            em.wait("scalar", lxs[slot], ldv)
                            sdone = emit_ln_stats(
                                em, xin[:, E * slot:E * (slot + 1)], lnSq,
                                pset[0], pset[1], pset[2], pset[3], ceps)
                            if pend is not None:
                                do_apply(*pend)
                            pend = (i, slot, pset, sdone)
                        do_apply(*pend)
                        xl_ready = em.val("vs")
                        if which == "kq":
                            for m in range(NEC):
                                slot = m % 2
                                kp = stqA[:, 512 * slot:512 * (slot + 1)]
                                em.wait("tensor", "vs", xl_ready)
                                em.wait("tensor", "vs", kq_evac[slot])
                                for k in range(m, NEC):
                                    em.op("tensor", lambda e, m=m, k=k, kp=kp,
                                          xo=xo: e.matmul(
                                              kp,
                                              wk_sb[:, E * k + NP * m:
                                                    E * k + NP * (m + 1)],
                                              xlT[:, xo + 512 * k:
                                                  xo + 512 * (k + 1)],
                                              start=(k == m), stop=(k == NEC - 1)),
                                          "ts" if k == NEC - 1 else None)
                                tdone = em.val("ts")
                                em.wait("vector", "ts", tdone)
                                em.op("vector", lambda e, m=m, g=g, kp=kp:
                                      e.tensor_copy(
                                          kT[:, T * m + 512 * g:
                                             T * m + 512 * (g + 1)], kp), "vs")
                                kq_evac[slot] = em.val("vs")
                                if g < 4:
                                    qp = stqB[:, 512 * slot:512 * (slot + 1)]
                                    em.wait("tensor", "vs", q_evac[slot])
                                    for k in range(0, m + 1):
                                        em.op("tensor",
                                              lambda e, m=m, k=k, qp=qp, xo=xo:
                                              e.matmul(
                                                  qp,
                                                  wq_sb[:, E * k + NP * m:
                                                        E * k + NP * (m + 1)],
                                                  xlT[:, xo + 512 * k:
                                                      xo + 512 * (k + 1)],
                                                  start=(k == 0), stop=(k == m)),
                                              "ts" if k == m else None)
                                    tdone = em.val("ts")
                                    em.wait("vector", "ts", tdone)
                                    em.op("vector", lambda e, m=m, g=g, qp=qp:
                                          e.tensor_copy(
                                              qT[:, Q * m + 512 * g:
                                                 Q * m + 512 * (g + 1)], qp),
                                          "vs")
                                    q_evac[slot] = em.val("vs")
                            em.wait("gpsimd", "vs", em.val("vs"))
                            em.op("gpsimd", lambda e, g=g: e.dma_start(
                                out=xls[0:NEC, :, 512 * g:512 * (g + 1)]
                                .transpose([1, 0, 2]),
                                in_=xlT[:, 0:4 * E]
                                .rearrange("p (c q) -> p c q", q=512)), "gww",
                                16)
                            nW = 1 if g < 7 else len(wjobs) - widx[0]
                            for _ in range(nW):
                                if widx[0] < len(wjobs):
                                    emit_wtile(widx[0], tpst)
                                    widx[0] += 1
                            xlT_used[gs_] = max(em.val("ts"), 0)
        
                        else:
                            for t in range(4):
                                for n in range(2):
                                    slot = n
                                    vp = (avP if slot == 0 else zP)[:, 0:512]
                                    em.wait("tensor", "vs", xl_ready)
                                    em.wait("tensor", "vs", v_evac[slot])
                                    for k in range(NEC):
                                        em.op("tensor",
                                              lambda e, t=t, n=n, k=k, vp=vp,
                                              xo=xo: e.matmul(
                                                  vp,
                                                  xlT[:, xo + 512 * k + NP * t:
                                                      xo + 512 * k + NP * (t + 1)],
                                                  wv_sb[:, E * k + 512 * n:
                                                        E * k + 512 * (n + 1)],
                                                  start=(k == 0),
                                                  stop=(k == NEC - 1)),
                                              "ts" if k == NEC - 1 else None)
                                    tdone = em.val("ts")
                                    em.wait("vector", "ts", tdone)
                                    em.op("vector",
                                          lambda e, g=g, t=t, n=n, vp=vp:
                                          e.tensor_copy(
                                              vS[:, E * (4 * g + t) + 512 * n:
                                                 E * (4 * g + t) + 512 * (n + 1)],
                                              vp), "vs")
                                    v_evac[slot] = em.val("vs")
                            xlT_used[gs_] = em.val("ts")
                    flush_block(nc, em, sems)

            emit_proj_phase("kq")
            esV = esPA.enter_context(ExitStack())
            vS = sb(esV, "vS", [NP, NTT * E], BF16)
            with ExitStack() as esP2:
                wv_sb = sb(esP2, "wv_sb", [NP, NEC * E], BF16)
                xlT2 = sb(esP2, "xlT2", [NP, 2 * 4 * E], BF16)
                for p in range(NEC):
                    em.op("gpsimd", lambda e, p=p: e.dma_start(
                        out=wv_sb[:, E * p:E * (p + 1)],
                        in_=wvT[NP * p:NP * (p + 1), :]), "gw", 16)
                em.wait("tensor", "gw", em.val("gw"))
                em.wait_now("sync", "gww")
                v2_evac = [0, 0]
                xl2_used = [0, 0]
                for g in range(NEC):
                    gsl = g % 2
                    xo2 = 4 * E * gsl
                    em.wait("sync", "ts", xl2_used[gsl])
                    em.op("sync", lambda e, g=g, xo2=xo2: e.dma_start(
                        out=xlT2[:, xo2:xo2 + 4 * E]
                        .rearrange("p (c q) -> p c q", q=512),
                        in_=xls[0:NEC, :, 512 * g:512 * (g + 1)]
                        .transpose([1, 0, 2])), "lx0", 16)
                    ldv = em.val("lx0")
                    em.wait("tensor", "lx0", ldv)
                    for t in range(4):
                        for n in range(2):
                            vslot = n
                            vp = (avP if vslot == 0 else zP)[:, 0:512]
                            em.wait("tensor", "vs", v2_evac[vslot])
                            for k in range(NEC):
                                em.op("tensor",
                                      lambda e, t=t, n=n, k=k, vp=vp, xo2=xo2:
                                      e.matmul(
                                          vp,
                                          xlT2[:, xo2 + 512 * k + NP * t:
                                              xo2 + 512 * k + NP * (t + 1)],
                                          wv_sb[:, E * k + 512 * n:
                                                E * k + 512 * (n + 1)],
                                          start=(k == 0),
                                          stop=(k == NEC - 1)),
                                      "ts" if k == NEC - 1 else None)
                            tdone = em.val("ts")
                            em.wait("vector", "ts", tdone)
                            em.op("vector",
                                  lambda e, g=g, t=t, n=n, vp=vp:
                                  e.tensor_copy(
                                      vS[:, E * (4 * g + t) + 512 * n:
                                         E * (4 * g + t) + 512 * (n + 1)],
                                      vp), "vs")
                            v2_evac[vslot] = em.val("vs")
                    xl2_used[gsl] = em.val("ts")
                flush_block(nc, em, sems)

            with ExitStack() as esA:
                wu_sb = sb(esA, "wu_sb", [NP, NEC * E], BF16)
                ptile = sb(esA, "ptile", [NP, 2 * 1024], BF16)
                aT = sb(esA, "aT", [NP, NEC * QT], BF16)
                zrs = sb(esA, "zrs", [NP, QT])
                zln = sb(esA, "zln", [33, QT])
                zrec = sb(esA, "zrec", [33, QT])
                bubc = sb(esA, "bubc", [NP, E])
                xr_sb = sb(esA, "xr_sb", [NP, E])
                xrb_sb = sb(esA, "xrb_sb", [NP, E])
                x2_sb = xr_sb
                em.op("gpsimd", lambda e: e.memset(zrec[:], 0.0), "gs")
                zrec_zeroed = em.val("gs")
                em.wait("tensor", "gs", zrec_zeroed)
                em.wait_now("gpsimd", "gww")
                for lo, gg in ((wg_loc, wgg), (wu_loc, wug), (wd_loc, wdg)):
                    em.op("gpsimd", lambda e, lo=lo, gg=gg:
                          e.collective_compute(
                              "AllGather", ALU.bypass,
                              replica_groups=[list(range(8))],
                              ins=[lo.opt()],
                              outs=[gg.opt()]), "gs")
                em.op("gpsimd", lambda e: e.dma_start(
                    out=bubc[:], in_=bu[0:1, :].broadcast_to([NP, E])),
                    "gw", 16)
                for p in range(NEC):
                    em.op("gpsimd", lambda e, p=p: e.dma_start(
                        out=wu_sb[:, E * p:E * (p + 1)],
                        in_=wuT[NP * p:NP * (p + 1), :]), "gw", 16)
                wu_ld = em.val("gw")
                bubc_ld = wu_ld

                em.op("vector", lambda e: e.memset(zP[:, 0:512], 1.0), "vs")
                zp_init = em.val("vs")
                exp_cnt = [0, 0]
                avz_cnt = [0, 0]
                aT_evac = zp_init
                zln_used = 0
                zrep_mm_prev = 0
                uproj_done = 0
                x2_done = 0
                for qt in range(NQTL):
                    em.wait("tensor", "vs", x2_done)
                    for pr in range(NPAIR):
                        def emit_qk(c):
                            buf = c % 2
                            stq = (stqA if buf == 0 else stqB)[:, 0:1024]
                            em.wait("tensor", "ss", exp_cnt[buf])
                            em.op("tensor", lambda e, pr=pr, c=c, qt=qt, stq=stq:
                                  e.matmul(stq[0:NP, 0:512],
                                           kT[0:64, T * pr + NP * c:
                                              T * pr + NP * (c + 1)],
                                           qT[0:64, Q * pr + QT * qt:
                                              Q * pr + QT * (qt + 1)],
                                           start=True, stop=True,
                                           tile_position=(0, 0)), None)
                            em.op("tensor", lambda e, pr=pr, c=c, qt=qt, stq=stq:
                                  e.matmul(stq[0:NP, 512:1024],
                                           kT[64:128, T * pr + NP * c:
                                              T * pr + NP * (c + 1)],
                                           qT[64:128, Q * pr + QT * qt:
                                              Q * pr + QT * (qt + 1)],
                                           start=True, stop=True,
                                           tile_position=(64, 0)), "ts")
                            return em.val("ts")

                        def emit_exp(c, qk_done):
                            buf = c % 2
                            stq = (stqA if buf == 0 else stqB)[:, 0:1024]
                            em.wait("scalar", "ts", qk_done)
                            em.wait("scalar", "ts", avz_cnt[buf])
                            em.op("scalar", lambda e, buf=buf, stq=stq:
                                  e.activation(
                                      ptile[:, 1024 * buf:1024 * (buf + 1)],
                                      stq[:, 0:1024], AF.Exp,
                                      bias=cbexp[:], scale=csexp[:]), "ss")
                            exp_cnt[buf] = em.val("ss")

                        def emit_avz(c):
                            buf = c % 2
                            em.wait("tensor", "ss", exp_cnt[buf])
                            if c == 0:
                                em.wait("tensor", "vs", aT_evac)
                                em.wait("tensor", "ss", zln_used)
                            em.op("tensor", lambda e, pr=pr, c=c, buf=buf:
                                  e.matmul(avP[0:64, 0:512],
                                           vS[:, E * c + 64 * (2 * pr):
                                              E * c + 64 * (2 * pr) + 64],
                                           ptile[:, 1024 * buf:1024 * buf + 512],
                                           start=(c == 0), stop=(c == NTT - 1),
                                           tile_position=(0, 0),
                                           skip_group_check=True), None)
                            em.op("tensor", lambda e, pr=pr, c=c, buf=buf:
                                  e.matmul(avP[64:128, 0:512],
                                           vS[:, E * c + 64 * (2 * pr + 1):
                                              E * c + 64 * (2 * pr + 1) + 64],
                                           ptile[:, 1024 * buf + 512:
                                                 1024 * buf + 1024],
                                           start=(c == 0), stop=(c == NTT - 1),
                                           tile_position=(0, 64),
                                           skip_group_check=True), None)
                            em.op("tensor", lambda e, c=c, buf=buf:
                                  e.matmul(zP[0:1, 0:512],
                                           onesz[:, 0:1],
                                           ptile[:, 1024 * buf:1024 * buf + 512],
                                           start=(c == 0), stop=(c == NTT - 1),
                                           tile_position=(0, 0),
                                           skip_group_check=True), None)
                            em.op("tensor", lambda e, c=c, buf=buf:
                                  e.matmul(zP[32:33, 0:512],
                                           onesz[:, 1:2],
                                           ptile[:, 1024 * buf + 512:
                                                 1024 * buf + 1024],
                                           start=(c == 0), stop=(c == NTT - 1),
                                           tile_position=(0, 32),
                                           skip_group_check=True), "ts")
                            avz_cnt[buf] = em.val("ts")

                        qkd = emit_qk(0)
                        emit_exp(0, qkd)
                        for c in range(1, NTT):
                            qkd = emit_qk(c)
                            emit_avz(c - 1)
                            emit_exp(c, qkd)
                        emit_avz(NTT - 1)
                        em.wait("scalar", "ts", avz_cnt[(NTT - 1) % 2])
                        em.op("scalar", lambda e: e.activation(
                            zln[0:33, :], zP[0:33, 0:512], AF.Ln), "ss")
                        zln_used = em.val("ss")
                        em.wait_now("scalar", "ss")
                        em.wait("scalar", "ts", zrep_mm_prev)
                        em.op("scalar", lambda e: e.activation(
                            zrec[0:33, :], zln[0:33, :], AF.Exp,
                            scale=-1.0), "ss")
                        zr_done = em.val("ss")
                        em.wait("tensor", "ss", zr_done)
                        em.op("tensor", lambda e: e.matmul(
                            zrepP[:, 0:512], sel33[:], zrec[:, :],
                            start=True, stop=True), "ts")
                        zrep_mm = em.val("ts")
                        zrep_mm_prev = zrep_mm
                        em.wait("vector", "ts", zrep_mm)
                        em.op("vector", lambda e: e.tensor_copy(
                            zrs[:], zrepP[:, 0:512]), "vs")
                        em.wait_now("vector", "vs")
                        em.wait("vector", "ts", uproj_done)
                        em.op("vector", lambda e, pr=pr: e.tensor_tensor(
                            aT[:, QT * pr:QT * (pr + 1)],
                            avP[:, 0:512], zrs[:], ALU.mult), "vs")
                        aT_evac = em.val("vs")
                    for mt in range(4):
                        em.wait("sync", "vs", x2_done)
                        em.wait("sync", "sx2", em.val("sx2"))
                        em.op("sync", lambda e, qt=qt, mt=mt: e.dma_start(
                            out=xr_sb[:],
                            in_=xb[QT * qt + NP * mt:QT * qt + NP * (mt + 1), :]),
                            "lxr", 16)
                        lxr = em.val("lxr")
                        em.wait("vector", "lxr", lxr)
                        em.wait("vector", "gw", bubc_ld)
                        em.op("vector", lambda e: e.tensor_add(
                            xrb_sb[:], xr_sb[:], bubc[:]), "vs")
                        xrb_done = em.val("vs")
                        em.wait("tensor", "ss", exp_cnt[(NTT - 1) % 2])
                        em.wait("tensor", "vs", aT_evac)
                        em.wait("tensor", "vs", x2_done)
                        em.wait("tensor", "gw", wu_ld)
                        for n in range(2):
                            for p in range(NEC):
                                em.op("tensor", lambda e, p=p, n=n, mt=mt:
                                      e.matmul(
                                          stqA[:, 512 * n:512 * (n + 1)],
                                          aT[:, QT * p + NP * mt:
                                             QT * p + NP * (mt + 1)],
                                          wu_sb[:, E * p + 512 * n:
                                                E * p + 512 * (n + 1)],
                                          start=(p == 0), stop=(p == NEC - 1)),
                                      "ts" if (p == NEC - 1 and n == 1) else None)
                        uproj_done = em.val("ts")
                        em.wait("vector", "ts", uproj_done)
                        em.wait("vector", "vs", xrb_done)
                        em.wait("vector", "sx2", em.val("sx2"))
                        em.op("vector", lambda e: e.tensor_tensor(
                            x2_sb[:], stqA[:, 0:1024], xrb_sb[:],
                            ALU.add), "vs")
                        x2v = em.val("vs")
                        em.wait("sync", "vs", x2v)
                        em.op("sync", lambda e, qt=qt, mt=mt: e.dma_start(
                            out=x2s[QT * qt + NP * mt:QT * qt + NP * (mt + 1), :],
                            in_=x2_sb[:]), "sx2", 16)
                        x2_done = x2v
                flush_block(nc, em, sems)

        with ExitStack() as esF:
            wgq_sb = sb(esF, "wgq_sb", [NP, NEC * I], BF16)
            wuq_sb = sb(esF, "wuq_sb", [NP, NEC * I], BF16)
            wdq_sb = sb(esF, "wdq_sb", [NP, NIC * E], BF16)
            xf = sb(esF, "xf", [NP, 3 * E])
            xq = sb(esF, "xq", [NP, 2 * E], BF16)
            xqT = sb(esF, "xqT", [NP, E], BF16)
            gsb = sb(esF, "gsb", [NP, 512])
            hbuf = sb(esF, "hbuf", [NP, I])
            hq = sb(esF, "hq", [NP, I], BF16)
            hqT = sb(esF, "hqT", [NP, I], BF16)
            ysb = sb(esF, "ysb", [NP, E])
            qB = sb(esF, "fqB", [NP, 1408])
            qP = sb(esF, "fqP", [NP, 1408])
            qQ = sb(esF, "fqQ", [NP, 1408])
            qR = sb(esF, "fqR", [NP, 1408])
            qsgn = sb(esF, "fqsgn", [NP, 1408], BF16)
            qam = sb(esF, "fqam", [NP, 88])
            xl2 = qP[:, 0:E]
            qs = sb(esF, "fqs", [NP, 88])
            qr = sb(esF, "fqr", [NP, 88])

            def emit_wloads():
                em.wait_now("gpsimd", "gs")
                for p in range(NEC):
                    em.op("gpsimd", lambda e, p=p: e.dma_start(
                        out=wgq_sb[:, I * p:I * (p + 1)]
                        .rearrange("p2 (t c) -> p2 t c", c=NP),
                        in_=wgg[0:NIC, NP * p:NP * (p + 1), :]
                        .transpose([1, 0, 2])), "gww", 16)
                    em.op("gpsimd", lambda e, p=p: e.dma_start(
                        out=wuq_sb[:, I * p:I * (p + 1)]
                        .rearrange("p2 (t c) -> p2 t c", c=NP),
                        in_=wug[0:NIC, NP * p:NP * (p + 1), :]
                        .transpose([1, 0, 2])), "gww", 16)
                wgu = em.val("gww")
                for j in range(NIC):
                    em.op("gpsimd", lambda e, j=j: e.dma_start(
                        out=wdq_sb[:, E * j:E * (j + 1)]
                        .rearrange("p2 (t c) -> p2 t c", c=NP),
                        in_=wdg[0:NEC, NP * j:NP * (j + 1), :]
                        .transpose([1, 0, 2])), "gww", 16)
                return wgu, em.val("gww")

            def pe_warm():
                em.wait_now("tensor", "vs")
                em.op("tensor", lambda e: e.matmul(
                    zrepP[0:64, 0:64], identb[:, 0:64], identb[:, 0:64],
                    start=True, stop=True), None)

            NW = [512, 512, 512, 512, 512, 256]
            xf_used = [0, 0, 0]
            gu_used = [0, 0]
            tp_evac = 0
            y_done = 0
            hTT_prev = 0
            pend_y = None
            lxs = ["lx0", "lx1"]
            xq_read = [0, 0]
            xq_done = [0, 0]

            ld_pf = {}

            def stage1a_load(t):
                slot = t % 3
                em.wait("sync", "vs", xf_used[slot])
                em.op("sync", lambda e, t=t, slot=slot: e.dma_start(
                    out=xf[:, E * slot:E * (slot + 1)],
                    in_=x2s[NP * t:NP * (t + 1), :]), lxs[t % 2], 16)
                ld_pf[t] = em.val(lxs[t % 2])

            def stage1a_stats(t):
                slot = t % 3
                em.wait("vector", lxs[t % 2], ld_pf[t])
                em.wait("scalar", lxs[t % 2], ld_pf[t])
                return emit_ln_stats(em, xf[:, E * slot:E * (slot + 1)],
                                     qB, lnm, lnv, lnr1, lnr2, ceps)

            def stage1b(t, sdone):
                slot = t % 3
                emit_ln_apply(em, xf[:, E * slot:E * (slot + 1)], xl2,
                              lnm, lnr2, sdone)
                qslot = t % 2
                em.wait("vector", "ts", xq_read[qslot])
                emit_quant(em, xl2, xq[:, E * qslot:E * (qslot + 1)],
                           qB, qP, qQ, qR, qsgn, qam, qs, qr, E,
                           x_alias_p=True, warm_cb=pe_warm)
                xq_done[qslot] = em.val("vs")

            def emit_gu(n):
                w = NW[n]
                c0 = 512 * n
                ps = n % 2
                psT = stqA if ps == 0 else stqB
                gP = psT[:, 0:512]
                uP = psT[:, 512:1024]
                em.wait("tensor", "ss", xqT_done)
                em.wait("tensor", "vs", gu_used[ps])
                for k in range(NEC):
                    em.op("tensor", lambda e, k=k, c0=c0, w=w, gP=gP:
                          e.matmul(gP[:, 0:w],
                                   xqT[:, NP * k:NP * (k + 1)],
                                   wgq_sb[:, I * k + c0:I * k + c0 + w],
                                   start=(k == 0), stop=(k == NEC - 1)),
                          None)
                for k in range(NEC):
                    em.op("tensor", lambda e, k=k, c0=c0, w=w, uP=uP:
                          e.matmul(uP[:, 0:w],
                                   xqT[:, NP * k:NP * (k + 1)],
                                   wuq_sb[:, I * k + c0:I * k + c0 + w],
                                   start=(k == 0), stop=(k == NEC - 1)),
                          "ts" if k == NEC - 1 else None)
                gu_dones[n] = em.val("ts")

            def emit_h(n):
                nonlocal hTT_prev
                w = NW[n]
                c0 = 512 * n
                ps = n % 2
                psT = stqA if ps == 0 else stqB
                gP = psT[:, 0:512]
                uP = psT[:, 512:1024]
                em.wait("scalar", "ts", gu_dones[n])
                em.wait("scalar", "vs", hTT_prev)
                em.op("scalar", lambda e, w=w, gP=gP: e.activation(
                    gsb[:, 0:w], gP[:, 0:w], AF.Silu), "ss")
                sil = em.val("ss")
                em.wait("vector", "ss", sil)
                em.op("vector", lambda e, c0=c0, w=w, uP=uP:
                      e.tensor_tensor(hbuf[:, c0:c0 + w], gsb[:, 0:w],
                                      uP[:, 0:w], ALU.mult), "vs")
                gu_used[ps] = em.val("vs")
                hTT_prev = gu_used[ps]

            def flush_hd(pt):
                nonlocal tp_evac, dn_done, y_done
                for j0 in range(0, NIC, 8):
                    jn = min(8, NIC - j0)
                    em.wait("tensor", "vs", hq_done)
                    em.wait("tensor", "ss", tp_evac)
                    for jj in range(jn):
                        j = j0 + jj
                        em.op("tensor", lambda e, j=j, jj=jj: e.transpose(
                            tpP[:, NP * jj:NP * (jj + 1)],
                            hq[:, NP * j:NP * (j + 1)], identb[:]), "ts")
                    tdone = em.val("ts")
                    em.wait("scalar", "ts", tdone)
                    em.op("scalar", lambda e, j0=j0, jn=jn: e.activation(
                        hqT[:, NP * j0:NP * (j0 + jn)],
                        tpP[:, 0:NP * jn], AF.Copy), "ss")
                    tp_evac = em.val("ss")
                hqT_done = tp_evac
                em.wait("tensor", "ss", hqT_done)
                em.wait("tensor", "vs", y_done)
                em.wait("tensor", "gww", wq_ld)
                for j in range(NIC):
                    for n in range(2):
                        em.op("tensor", lambda e, j=j, n=n: e.matmul(
                            (avP if n == 0 else zP)[:, 0:512],
                            hqT[:, NP * j:NP * (j + 1)],
                            wdq_sb[:, E * j + 512 * n:E * j + 512 * (n + 1)],
                            start=(j == 0), stop=(j == NIC - 1)),
                            "ts" if (j == NIC - 1 and n == 1) else None)
                dn_done = em.val("ts")

            def flush_y(pt, pslot):
                nonlocal y_done
                em.wait("vector", "ts", dn_done)
                em.wait("vector", "sy", em.val("sy"))
                em.op("vector", lambda e, pslot=pslot: e.tensor_tensor(
                    ysb[:, 0:512], avP[:, 0:512],
                    xf[:, E * pslot:E * pslot + 512], ALU.add), "vs")
                em.wait_now("vector", "vs")
                em.op("vector", lambda e, pslot=pslot: e.tensor_tensor(
                    ysb[:, 512:1024], zP[:, 0:512],
                    xf[:, E * pslot + 512:E * pslot + 1024], ALU.add), "vs")
                y_done = em.val("vs")
                xf_used[pslot] = y_done
                em.wait("sync", "vs", y_done)
                em.op("sync", lambda e, pt=pt: e.dma_start(
                    out=y[NP * pt:NP * (pt + 1), :], in_=ysb[:]), "sy", 16)

            dn_done = 0
            hq_done = 0
            hqtp_done = 0
            pend = None
            em.wait_now("sync", "sx2")
            stage1a_load(0)
            stage1a_load(1)
            sd_next = stage1a_stats(0)
            stage1b(0, sd_next)
            wgu_ld, wq_ld = emit_wloads()
            for tt in range(NQT):
                slot = tt % 3
                qslot = tt % 2
                em.wait("tensor", "vs", xq_done[qslot])
                em.wait("tensor", "ss", tp_evac)
                for p in range(NEC):
                    em.op("tensor", lambda e, p=p, qslot=qslot: e.transpose(
                        tpP[:, NP * p:NP * (p + 1)],
                        xq[:, E * qslot + NP * p:E * qslot + NP * (p + 1)],
                        identb[:]), "ts")
                tdone = em.val("ts")
                xq_read[qslot] = tdone
                em.wait("scalar", "ts", tdone)
                em.op("scalar", lambda e: e.activation(
                    xqT[:], tpP[:, 0:E], AF.Copy), "ss")
                tp_evac = em.val("ss")
                xqT_done = tp_evac
                if tt + 1 < NQT:
                    sd_next = stage1a_stats(tt + 1)
                if pend is not None:
                    flush_hd(pend[0])
                    hqtp_done = em.val("ts")
                em.wait("tensor", "gww", wgu_ld)
                gu_dones = {}
                emit_gu(0)
                emit_gu(1)
                if tt + 1 < NQT:
                    stage1b(tt + 1, sd_next)
                if pend is not None:
                    flush_y(pend[0], pend[1])
                    pend = None
                if tt + 2 < NQT:
                    stage1a_load(tt + 2)
                for n in range(2, 8):
                    emit_h(n - 2)
                    if n < 6:
                        emit_gu(n)
                em.wait("vector", "ts", hqtp_done)
                emit_quant(em, hbuf[:, 0:1408], hq[:, 0:1408],
                           qB, qP, qQ, qR, qsgn, qam, qs, qr, 1408,
                           warm_cb=pe_warm)
                emit_quant(em, hbuf[:, 1408:2816], hq[:, 1408:2816],
                           qB, qP, qQ, qR, qsgn, qam, qs, qr, 1408,
                           warm_cb=pe_warm)
                hq_done = em.val("vs")
                pend = (tt, slot)
            flush_hd(pend[0])
            flush_y(pend[0], pend[1])
            em.wait_now("sync", "sy")
            flush_block(nc, em, sems)

        nc.finalize()
    return nc


_NC_CACHE = None


def kernel(L0_attn_x, L0_attn_Wk, L0_attn_Wq, L0_attn_Wv, L0_attn_Wu,
           L0_attn_bu, L0_ffn_w_gate, L0_ffn_w_up, L0_ffn_w_down):
    global _NC_CACHE
    x = np.asarray(L0_attn_x, dtype=np.float32)
    wkT = np.ascontiguousarray(np.asarray(L0_attn_Wk, np.float32).T)
    wqT = np.ascontiguousarray(np.asarray(L0_attn_Wq, np.float32).T)
    wvT = np.ascontiguousarray(np.asarray(L0_attn_Wv, np.float32).T)
    wuT = np.ascontiguousarray(np.asarray(L0_attn_Wu, np.float32).T)
    bu = np.ascontiguousarray(np.asarray(L0_attn_bu, np.float32)[None, :])
    wg = np.ascontiguousarray(np.asarray(L0_ffn_w_gate, np.float32))
    wu2 = np.ascontiguousarray(np.asarray(L0_ffn_w_up, np.float32))
    wd = np.ascontiguousarray(np.asarray(L0_ffn_w_down, np.float32))

    if _NC_CACHE is None:
        _NC_CACHE = build_nc()
    nc = _NC_CACHE

    in_maps = []
    for c in range(8):
        b, h = c // 2, c % 2
        xbat = x[b]
        xrot = np.ascontiguousarray(
            np.concatenate([xbat[Q * h:Q * (h + 1)], xbat[Q * (1 - h):Q * (2 - h)]],
                           axis=0))
        gt = [min(3 * c + i, 21) for i in range(3)]
        wgs = np.ascontiguousarray(
            np.concatenate([wg[NP * t:NP * (t + 1)] for t in gt], axis=0))
        wus = np.ascontiguousarray(
            np.concatenate([wu2[NP * t:NP * (t + 1)] for t in gt], axis=0))
        td = min(c, 7)
        wds = np.ascontiguousarray(wd[NP * td:NP * (td + 1)])
        in_maps.append({"xb": xrot, "wkT": wkT, "wqT": wqT, "wvT": wvT,
                        "wuT": wuT, "bu": bu, "wg": wg, "wu2": wu2, "wd": wd,
                        "wgs": wgs, "wus": wus, "wds": wds})
    res = run_bass_kernel_spmd(nc, in_maps, core_ids=list(range(8)))
    out = np.empty((4, T, E), dtype=np.float32)
    for c in range(8):
        b, h = c // 2, c % 2
        out[b, Q * h:Q * (h + 1), :] = res.results[c]["y"]
    return out
